# version 13 (final)
# baseline (speedup 1.0000x reference)
import math

import numpy as np
import ml_dtypes

import concourse.bass as bass
import concourse.mybir as mybir
from concourse.bass_utils import run_bass_kernel_spmd

F32 = mybir.dt.float32
BF16 = mybir.dt.bfloat16
I8 = mybir.dt.int8
I32 = mybir.dt.int32
AF = mybir.ActivationFunctionType
ALU = mybir.AluOpType
AX = mybir.AxisListType

HID = 2048
H4 = 4
T = 4096
NT = 8
TILE = 512
KS = 4
HF = 8192
TOK = 1024
LN128 = math.log(128.0)

ENGS = ("sync", "scalar", "vector", "tensor", "gpsimd")

_TAPS = []
_STOP = None
_NC_CACHE = {}


class _StopSched(Exception):
    pass


class Sched:
    def __init__(self, nc):
        self.nc = nc
        self.q = {e: [] for e in ENGS}
        self.cnt = {}
        self.sems = {}

    def emit(self, eng, build, incs=()):
        incs = tuple(incs)
        for name, amt in incs:
            self.cnt[name] = self.cnt.get(name, 0) + amt

        def thunk(e, _b=build, _i=incs, _s=self.sems):
            ins = _b(e)
            for name, amt in _i:
                ins = ins.then_inc(_s[name], amt)

        self.q[eng].append(thunk)

    def wait(self, eng, semname, val=None):
        v = self.cnt.get(semname, 0) if val is None else val
        if v <= 0:
            return

        def thunk(e, _n=semname, _v=v, _s=self.sems):
            e.wait_ge(_s[_n], _v)

        self.q[eng].append(thunk)

    def barrier(self, engines=ENGS,
                sems=("Lw", "Cw", "Cy", "Lx", "Lm", "Lpl", "Lu8", "Ld8", "Lc", "LT", "Gs", "Gp", "GpA", "GpB", "Gy", "GyA", "GyB", "QuA", "QuB", "QuC", "QuD", "QdA", "QdB", "QdC", "QdD", "LcA", "LcB", "STA", "STB",
                      "T", "V", "S", "ST")):
        for eng in engines:
            for s in sems:
                self.wait(eng, s)


class Arena:

    def __init__(self, ap, cols):
        self.ap = ap
        self.cols = cols
        self.off = 0
        self.marks = []
        self.peak = 0

    def alloc(self, n, dtype=BF16, name=""):
        if dtype == I8:
            cols = (n + 1) // 2
        elif dtype == F32:
            cols = 2 * n
        else:
            cols = n
        self.off = (self.off + 1) // 2 * 2
        assert self.off + cols <= self.cols, (
            f"arena overflow: {name} needs {cols} cols at {self.off}, cap {self.cols}")
        sl = self.ap[:, self.off:self.off + cols]
        self.off += cols
        self.peak = max(self.peak, self.off)
        if dtype != BF16:
            sl = sl.bitcast(dtype)
        return sl

    def view(self, off, n, dtype=BF16):
        cols = 2 * n if dtype == F32 else n
        sl = self.ap[:, off:off + cols]
        if dtype != BF16:
            sl = sl.bitcast(dtype)
        return sl

    def push(self):
        self.marks.append(self.off)

    def pop(self):
        self.off = self.marks.pop()


def build_nc(taps=()):
    nc = bass.Bass("TRN2", target_bir_lowering=False, debug=False)
    tap_aps = {}
    for name, shape, dtp in taps:
        tap_aps[name] = nc.dram_tensor(name, shape, dtp, kind="ExternalOutput").ap()

    dt = nc.dram_tensor
    xtm = dt("xtm", [T, HID], F32, kind="ExternalInput").ap()
    xtmj = dt("xtmj", [TOK, HID], F32, kind="ExternalInput").ap()
    xfmj = dt("xfmj", [HID, TOK], F32, kind="ExternalInput").ap()
    wq = dt("wq", [HID, 512], F32, kind="ExternalInput").ap()
    wk = dt("wk", [HID, 512], F32, kind="ExternalInput").ap()
    wv = dt("wv", [HID, 512], F32, kind="ExternalInput").ap()
    wg = dt("wg", [HID, 512], F32, kind="ExternalInput").ap()
    wab = dt("wab", [HID, 8], F32, kind="ExternalInput").ap()
    abb = dt("abb", [8, 1], F32, kind="ExternalInput").ap()
    convw = dt("convw", [128, 3 * 4 * KS], F32, kind="ExternalInput").ap()
    convb = dt("convb", [128, 3 * 4], F32, kind="ExternalInput").ap()
    lnwb = dt("lnwb", [128, 128], F32, kind="ExternalInput").ap()
    lnbb = dt("lnbb", [128, 128], F32, kind="ExternalInput").ap()
    woj = dt("woj", [512, HID], F32, kind="ExternalInput").ap()
    wup8 = dt("wup8", [HID, HF], I8, kind="ExternalInput").ap()
    wdn8 = dt("wdn8", [HF, HID], I8, kind="ExternalInput").ap()
    supt = dt("supt", [32, HF], F32, kind="ExternalInput").ap()
    sdnt = dt("sdnt", [128, HID], F32, kind="ExternalInput").ap()
    supt_bf = dt("supt_bf", [32, HF], BF16).ap()
    sel_up = dt("sel_up", [32, 2048], BF16, kind="ExternalInput").ap()
    sdnt_bf = dt("sdnt_bf", [128, HID], BF16).ap()
    bupt = dt("bupt", [128, 64], F32, kind="ExternalInput").ap()
    bdnt = dt("bdnt", [128, 16], F32, kind="ExternalInput").ap()
    ula = dt("ula", [HID, 16], F32, kind="ExternalInput").ap()
    ulb = dt("ulb", [16, HF], F32, kind="ExternalInput").ap()
    dla = dt("dla", [HF, 16], F32, kind="ExternalInput").ap()
    dlb = dt("dlb", [16, HID], F32, kind="ExternalInput").ap()
    identb = dt("identb", [128, 128], BF16, kind="ExternalInput").ap()
    identf = dt("identf", [128, 128], F32, kind="ExternalInput").ap()
    tuones = dt("tuones", [128, 128], F32, kind="ExternalInput").ap()
    masksu4 = dt("masksu4", [128, 512], BF16, kind="ExternalInput").ap()
    maskiu4 = dt("maskiu4", [128, 512], BF16, kind="ExternalInput").ap()
    identb4 = dt("identb4", [128, 512], BF16, kind="ExternalInput").ap()

    out_d = dt("out", [HID, TOK], F32, kind="ExternalOutput").ap()

    lgbt_dram = dt("lgbt_dram", [128, 128], F32).ap()
    cumt_dram = dt("cumt_dram", [128, 128], F32).ap()
    clast_dram = dt("clast_dram", [1, 128], F32).ap()
    rs_in = dt("rs_in", [4 * HID, TOK], BF16).ap()
    rs_out = dt("rs_out", [HID, TOK], BF16).ap()
    y2_dram = dt("y2_dram", [HID, TOK], F32).ap()

    def ckpt(name):
        if _STOP == name:
            raise _StopSched

    from contextlib import ExitStack
    with ExitStack() as cx:
        ACOLS = 88320
        abf = Arena(cx.enter_context(nc.sbuf_tensor("abf", [128, ACOLS], BF16))[:, :],
                    ACOLS)

        def sb(name, shape, dtp):
            return cx.enter_context(nc.sbuf_tensor(name, shape, dtp))[
                tuple(slice(None) for _ in shape)]

        c_identb = sb("c_identb", [128, 128], BF16)
        c_identf = sb("c_identf", [128, 128], F32)
        c_tuones = sb("c_tuones", [128, 128], F32)
        c_lnw = sb("c_lnw", [128, 128], F32)
        c_lnb = sb("c_lnb", [128, 128], F32)
        pl_cum = sb("pl_cum", [128, 128], F32)
        pl_ncumc = sb("pl_ncumc", [128, 128], F32)
        pl_ncumc2 = sb("pl_ncumc2", [128, 128], F32)
        pl_gam = sb("pl_gam", [128, 128], F32)
        pl_lgb = sb("pl_lgb", [128, 128], F32)
        pl_nexplgb = sb("pl_nexplgb", [128, 128], F32)
        pl_btm = sb("pl_btm", [128, 128], F32)
        pl_clb = sb("pl_clb", [128, 128], F32)
        pl_grev = sb("pl_grev", [128, 128], F32)
        pl_gamc = sb("pl_gamc", [128, 128], F32)
        ab_sig = sb("ab_sig", [8, 512], F32)
        ab_ln = sb("ab_ln", [8, 512], F32)
        la_tile = sb("la_tile", [128, 16], F32)
        pl_scr = sb("pl_scr", [128, 16], F32)
        lb_tile = sb("lb_tile", [128, 16], F32)
        abbv = sb("abbv", [8, 1], F32)
        cw_f = sb("cw_f", [128, 3 * 4 * KS], F32)
        cb_f = sb("cb_f", [128, 3 * 4], F32)
        plstage = sb("plstage", [16, 128], F32)
        stats = sb("stats", [128, 48], F32)
        dlb_bf = sb("dlb_bf", [16, HID], BF16)
        l1_bf = sb("l1_bf", [16, TOK], BF16)
        l2_f = sb("l2_f", [16, TOK], F32)
        l2_tmp = sb("l2_tmp", [16, TOK], F32)
        l2_bf = sb("l2_bf", [16, TOK], BF16)
        ulab = sb("ulab", [128, 16 * 16], BF16)
        ulbb = sb("ulbb", [16, 1024], BF16)
        dlab = sb("dlab", [128, 16 * 64], BF16)
        bupv = sb("bupv", [128, 64], F32)
        bdnv = sb("bdnv", [128, 16], F32)
        fconst = sb("fconst", [128, 4], F32)

        ps = [cx.enter_context(nc.psum_tensor(f"ps{i}", [128, 512], F32))[:, :]
              for i in range(6)]
        pft = cx.enter_context(nc.psum_tensor("pft", [128, 512], F32))[:, :]
        pbt = cx.enter_context(nc.psum_tensor("pbt", [128, 1024], BF16))[:, :]

        S = Sched(nc)
        for s in ("Lw", "Cw", "Cy", "Lx", "Lm", "Lpl", "Lu8", "Ld8", "Lc", "LT", "Gs", "Gp", "GpA", "GpB", "Gy", "GyA", "GyB", "QuA", "QuB", "QuC", "QuD", "QdA", "QdB", "QdC", "QdD", "LcA", "LcB", "STA", "STB",
                  "T", "V", "S", "CC", "ST"):
            S.cnt[s] = 0
            S.sems[s] = cx.enter_context(nc.semaphore(s))

        def dma(eng, out, in_, sem):
            S.emit(eng, lambda e, _o=out, _i=in_: e.dma_start(out=_o, in_=_i),
                   [(sem, 16)])

        def dma_acc(out, in_, sem):
            S.emit("gpsimd",
                   lambda e, _o=out, _i=in_: e.dma_start(out=_o, in_=_i,
                                                         accum_op=ALU.add),
                   [(sem, 16)])

        def hbcast(out, in_row, parts, sem):
            S.emit("sync",
                   lambda e, _o=out, _i=in_row, _p=parts:
                   e.dma_start(out=_o, in_=_i.partition_broadcast(_p)), [(sem, 16)])

        def bcast(out, in_row, parts, sem):
            S.emit("gpsimd",
                   lambda e, _o=out, _i=in_row, _p=parts:
                   e.dma_start(out=_o, in_=_i.partition_broadcast(_p)), [(sem, 16)])

        def mm(out, lhsT, rhs, start, stop, inc=False):
            S.emit("tensor",
                   lambda e, _o=out, _l=lhsT, _r=rhs, _s=start, _p=stop:
                   e.matmul(_o, _l, _r, start=_s, stop=_p),
                   [("T", 1)] if inc else ())

        def trans(out, in_, ident, inc=False):
            S.emit("tensor",
                   lambda e, _o=out, _i=in_, _d=ident: e.transpose(_o, _i, _d),
                   [("T", 1)] if inc else ())

        def act(out, in_, func, bias=0.0, scale=1.0, accum=None, inc=True):
            S.emit("scalar",
                   lambda e, _o=out, _i=in_, _f=func, _b=bias, _s=scale, _a=accum:
                   e.activation(_o, _i, _f, bias=_b, scale=_s, accum_out=_a),
                   [("S", 1)] if inc else ())

        def vcopy(out, in_, inc=True):
            S.emit("vector", lambda e, _o=out, _i=in_: e.tensor_copy(out=_o, in_=_i),
                   [("V", 1)] if inc else ())

        def vtt(op, out, a, b, inc=True):
            S.emit("vector",
                   lambda e, _o=out, _a=a, _b=b, _p=op:
                   e.tensor_tensor(out=_o, in0=_a, in1=_b, op=_p),
                   [("V", 1)] if inc else ())

        def vts(out, in0, s1, op0, s2=None, op1=None, inc=True):
            if op1 is None:
                S.emit("vector",
                       lambda e, _o=out, _i=in0, _s1=s1, _o0=op0:
                       e.tensor_scalar(out=_o, in0=_i, scalar1=_s1, scalar2=None,
                                       op0=_o0),
                       [("V", 1)] if inc else ())
            else:
                S.emit("vector",
                       lambda e, _o=out, _i=in0, _s1=s1, _s2=s2, _o0=op0, _o1=op1:
                       e.tensor_scalar(out=_o, in0=_i, scalar1=_s1, scalar2=_s2,
                                       op0=_o0, op1=_o1),
                       [("V", 1)] if inc else ())

        def vstt(out, in0, scalar, in1, op0, op1, inc=True):
            S.emit("vector",
                   lambda e, _o=out, _i0=in0, _sc=scalar, _i1=in1, _o0=op0, _o1=op1:
                   e.scalar_tensor_tensor(out=_o, in0=_i0, scalar=_sc, in1=_i1,
                                          op0=_o0, op1=_o1),
                   [("V", 1)] if inc else ())

        def vsw():
            S.wait("vector", "V")

        def ssw():
            S.wait("scalar", "S")

        def vmemset(ap, val, inc=True):
            S.emit("vector", lambda e, _a=ap, _v=val: e.memset(_a, _v),
                   [("V", 1)] if inc else ())

        def vred(out, in_, inc=True):
            S.emit("vector",
                   lambda e, _o=out, _i=in_: e.reduce_sum(out=_o, in_=_i, axis=AX.X),
                   [("V", 1)] if inc else ())

        try:
            abf.push()
            w_bf = {m: [abf.alloc(512, name=f"w{m}_{k}") for k in range(16)]
                    for m in range(4)}
            wab_bf = [abf.alloc(8, name=f"wab{k}") for k in range(16)]
            conv_diag = {m: [abf.alloc(512, name=f"cd{m}{oc}") for oc in range(4)]
                         for m in range(3)}
            convin = {m: [abf.alloc(516, name=f"ci{m}{oc}") for oc in range(4)]
                      for m in range(3)}
            g_sig = [abf.alloc(512, name=f"gs{oc}") for oc in range(4)]
            xn_off = abf.off
            xn_fm = abf.alloc(16 * 512, name="xn_fm")
            mask_su4 = abf.view(xn_off, 512)
            mask_iu4 = abf.view(xn_off + 512, 512)
            identb4w = abf.view(xn_off + 1024, 512)
            tmpAw = abf.view(xn_off + 1536, 512, F32)
            tmpBw = abf.view(xn_off + 2560, 512, F32)
            woj_bf = [abf.alloc(2048, name=f"woj{k}") for k in range(4)]
            og_tile = [abf.alloc(512, name=f"ogt{h}") for h in range(4)]
            rs_stage = [abf.alloc(512, name=f"rss{i}") for i in range(2)]
            xn_tok_bf = abf.alloc(2048, name="xn_tok_bf")
            x_tok = abf.alloc(2048, F32, name="x_tok")

            for src, dst in ((identb, c_identb), (identf, c_identf), (tuones, c_tuones),
                             (lnwb, c_lnw),
                             (lnbb, c_lnb), (abb, abbv), (convw, cw_f), (convb, cb_f),
                             (bupt, bupv), (bdnt, bdnv)):
                dma("sync", dst, src, "Lw")

            for k in range(16):
                dma("gpsimd", wab_bf[k], wab[128 * k:128 * k + 128, :], "Cw")
            for m, wsrc in enumerate((wq, wk, wv, wg)):
                for k in range(16):
                    dma("gpsimd", w_bf[m][k], wsrc[128 * k:128 * k + 128, :], "Cw")
            for k in range(4):
                dma("gpsimd", woj_bf[k], woj[128 * k:128 * k + 128, :], "Cy")
            for k in range(64):
                dma("gpsimd", dlab[:, 16 * k:16 * k + 16],
                    dla[128 * k:128 * k + 128, :], "Cy")
            for k in range(16):
                dma("gpsimd", ulab[:, 16 * k:16 * k + 16],
                    ula[128 * k:128 * k + 128, :], "Cy")
            dma("gpsimd", dlb_bf, dlb, "Cy")
            dma("gpsimd", supt_bf, supt, "Cy")
            dma("gpsimd", sdnt_bf, sdnt, "Cy")

            S.wait("vector", "Lw")
            for fi, fv in enumerate((1e-5, -7.5, -0.5 * LN128, -0.5)):
                S.emit("vector", lambda e, _a=fconst[:, fi:fi + 1], _v=fv: e.memset(_a, _v),
                       [("V", 1)])
            nc.const_aps.register(F32, -7.5, fconst[:, 1:2])
            for m in range(3):
                for oc in range(4):
                    for j in range(KS):
                        vts(conv_diag[m][oc][:, 128 * j:128 * j + 128], c_identf,
                            cw_f[:, (m * 4 + oc) * KS + j:(m * 4 + oc) * KS + j + 1],
                            ALU.mult, inc=False)
                    vmemset(convin[m][oc][:, 0:3], 0.0, inc=False)
                    vmemset(convin[m][oc][:, 515:516], 0.0, inc=False)
            vmemset(la_tile, 0.0)

            s_f32 = [[abf.alloc(128, F32, name=f"S{h}_{p}") for p in range(2)]
                     for h in range(H4)]
            s_bf = [abf.alloc(128, name=f"Sb{h}") for h in range(H4)]
            def wide4(name):
                base = abf.alloc(512, name=name)
                return base, [base[:, 128 * h:128 * h + 128] for h in range(H4)]

            a4w, a4 = wide4("a4")
            abarw, abar4 = wide4("ab4")
            xtw, xt4 = wide4("xt4")
            vtmw, vtm4 = wide4("vt4")
            kgw, kg4 = wide4("kg4")
            q0w, q04 = wide4("q04")
            pjw, pj4 = wide4("pj4")
            qjw, qj4 = wide4("qj4")
            qlevW = [abf.alloc(512, name=f"ql4_{i}") for i in range(4)]
            ybw = [abf.alloc(512, name=f"yb4_{i}") for i in range(2)]
            rbfw, rbf4 = a4w, a4
            n34 = q04
            w4w, w4 = wide4("w4")
            ebpQ = abf.alloc(512, F32, name="ebpQ")
            eapQ = abf.alloc(512, F32, name="eapQ")
            orw4 = [abf.alloc(128, F32, name=f"orw4_{h}") for h in range(H4)]
            rf2 = [abf.alloc(128, F32, name=f"rf2_{i}") for i in range(2)]
            otmp1 = abf.alloc(128, F32, name="otmp1")
            n1b1 = abf.alloc(128, F32, name="n1b1")
            n2b1 = abf.alloc(128, F32, name="n2b1")
            dclA = rf2
            dclB = [n1b1, n2b1]
            tmpA = dclA
            tmpB = dclB
            sjunk = otmp1

            for h in range(H4):
                vmemset(s_f32[h][0], 0.0, inc=False)
                vmemset(s_f32[h][1], 0.0, inc=False)
                vmemset(s_bf[h], 0.0, inc=(h == H4 - 1))

            cap = {}

            def ln_subtile(x_src_ap):
                S.wait("sync", "V", cap.get("pbt_free"))
                S.wait("sync", "S", cap.get("ln_s"))
                dma("sync", x_tok, x_src_ap, "Lx")
                S.wait("scalar", "Lx")
                S.wait("vector", "Lx")
                act(xn_tok_bf, x_tok, AF.Square, accum=stats[:, 0:1])
                vred(stats[:, 1:2], x_tok)
                S.wait("vector", "S")
                vsw()
                vts(stats[:, 2:3], stats[:, 1:2], 1.0 / 2048, ALU.mult, inc=False)
                vts(stats[:, 3:4], stats[:, 0:1], 1.0 / 2048, ALU.mult)
                vsw()
                vts(stats[:, 4:5], stats[:, 2:3], stats[:, 2:3], ALU.mult)
                vsw()
                vtt(ALU.subtract, stats[:, 5:6], stats[:, 3:4], stats[:, 4:5])
                S.wait("scalar", "V")
                act(stats[:, 6:7], stats[:, 5:6], AF.Ln, bias=fconst[:, 0:1])
                ssw()
                act(stats[:, 7:8], stats[:, 6:7], AF.Exp, scale=fconst[:, 3:4])
                S.wait("vector", "S")
                vts(xn_tok_bf, x_tok, stats[:, 2:3], ALU.subtract, stats[:, 7:8], ALU.mult)
                cap["ln_v"] = S.cnt["V"]
                cap["ln_s"] = S.cnt["S"]

            def xn_transpose(dst_3d):
                for half in range(2):
                    S.wait("tensor", "V", cap.get("pbt_free"))
                    S.wait("tensor", "V", cap.get("ln_v"))
                    for kk in range(8):
                        k = half * 8 + kk
                        trans(pbt[:, 128 * kk:128 * kk + 128],
                              xn_tok_bf[:, 128 * k:128 * k + 128], c_identb, inc=(kk == 7))
                    capt = S.cnt["T"]
                    S.wait("vector", "T", capt)
                    src = pbt.rearrange("p (c f) -> p c f", f=128)
                    vcopy(dst_3d[:, 8 * half:8 * half + 8, :], src)
                    cap["pbt_free"] = S.cnt["V"]

            def planes_for_tile(ti):
                col0 = 16 * ti
                sl = slice(col0, col0 + 16)
                S.wait("tensor", "V", cap.get("pbt_free"))
                S.wait("tensor", "S", cap.get("planes_pft_s"))
                for k in range(16):
                    mm(ps[2][0:8, :], wab_bf[k], xn_fm[:, 512 * k:512 * k + 512],
                       start=(k == 0), stop=(k == 15), inc=(k == 15))
                S.wait("scalar", "T")
                act(ab_sig, ps[2][0:8, :], AF.Sigmoid, bias=abbv)
                ssw()
                act(ab_ln, ab_sig, AF.Ln)
                S.wait("tensor", "S")
                for c2 in range(4):
                    trans(pft[0:128, 16 * c2:16 * c2 + 8],
                          ab_ln[:, 128 * c2:128 * c2 + 128], c_identf[0:8, 0:8], inc=False)
                    trans(pft[0:128, 16 * c2 + 8:16 * c2 + 16],
                          ab_sig[:, 128 * c2:128 * c2 + 128], c_identf[0:8, 0:8],
                          inc=(c2 == 3))
                S.wait("vector", "T")
                for c2 in range(4):
                    base = 16 * c2
                    vcopy(la_tile[:, 4 * c2:4 * c2 + 4], pft[0:128, base:base + 4],
                          inc=False)
                    vcopy(lb_tile[:, 4 * c2:4 * c2 + 4], pft[0:128, base + 4:base + 8],
                          inc=False)
                    vcopy(pl_btm[:, col0 + 4 * c2:col0 + 4 * c2 + 4],
                          pft[0:128, base + 12:base + 16], inc=(c2 == 3))
                S.wait("tensor", "V")
                mm(ps[2][:, 0:16], c_tuones, la_tile, start=True, stop=True, inc=True)
                S.wait("scalar", "T")
                act(pl_cum[:, sl], ps[2][:, 0:16], AF.Copy)
                if ti == 0 and "t_cum0" in tap_aps:
                    S.wait("sync", "S")
                    dma("sync", tap_aps["t_cum0"], pl_cum[:, 0:16], "LT")
                    dma("sync", tap_aps["t_latile"], la_tile, "LT")
                    dma("sync", tap_aps["t_tu"], c_tuones, "LT")
                    S.wait("vector", "LT")
                    S.wait("scalar", "LT")
                    S.wait("tensor", "LT")
                S.wait("vector", "S")
                vts(pl_ncumc[:, sl], pl_cum[:, sl], -1.0, ALU.mult, -LN128, ALU.add,
                    inc=False)
                vts(pl_ncumc2[:, sl], pl_cum[:, sl], -1.0, ALU.mult, -0.5 * LN128, ALU.add,
                    inc=False)
                vtt(ALU.subtract, pl_scr, pl_cum[:, sl], la_tile)
                vsw()
                vtt(ALU.add, pl_lgb[:, sl], pl_scr, lb_tile)
                S.wait("scalar", "V")
                act(pl_gam[:, sl], pl_cum[:, sl], AF.Exp, inc=False)
                act(pl_scr, pl_lgb[:, sl], AF.Exp, bias=fconst[:, 2:3])
                S.wait("vector", "S")
                vts(pl_nexplgb[:, sl], pl_scr, -1.0, ALU.mult)
                S.wait("gpsimd", "S")
                dma("gpsimd", clast_dram[0:1, sl], pl_cum[127:128, sl], "Gp")
                S.wait("gpsimd", "Gp")
                bcast(pl_clb[:, sl], clast_dram[0:1, sl], 128, "Gp")
                S.wait("vector", "Gp")
                vsw()
                vtt(ALU.subtract, pl_scr, pl_clb[:, sl], pl_cum[:, sl])
                S.wait("scalar", "V")
                act(pl_grev[:, sl], pl_scr, AF.Exp, bias=fconst[:, 2:3], inc=False)
                act(pl_gamc[:, sl], pl_clb[:, sl], AF.Exp)
                for plane, dest in ((pl_lgb, lgbt_dram), (pl_cum, cumt_dram)):
                    S.wait("tensor", "S")
                    S.wait("tensor", "V")
                    trans(pft[0:16, 0:128], plane[:, sl], c_identf, inc=True)
                    S.wait("vector", "T")
                    S.wait("vector", "Lpl")
                    vcopy(plstage, pft[0:16, 0:128])
                    S.wait("sync", "V")
                    dma("sync", dest[col0:col0 + 16, :], plstage, "Lpl")
                cap["lpl_rows"] = S.cnt["Lpl"]
                cap["planes_pft_v"] = S.cnt["V"]
                cap["planes_pft_s"] = S.cnt["S"]

            def proj_and_conv(ti):
                for m in range(4):
                    for oc in range(4):
                        g = m * 4 + oc
                        bank = ps[g % 2]
                        S.wait("tensor", "S", cap.get(f"pj{g % 2}"))
                        if g == 0:
                            S.wait("tensor", "S")
                        for k in range(16):
                            mm(bank, w_bf[m][k][:, 128 * oc:128 * oc + 128],
                               xn_fm[:, 512 * k:512 * k + 512],
                               start=(k == 0), stop=(k == 15), inc=(k == 15))
                        cap[f"pjm{g % 2}"] = S.cnt["T"]
                        S.wait("scalar", "T", cap[f"pjm{g % 2}"])
                        if m < 3:
                            act(convin[m][oc][:, 3:515], bank, AF.Copy)
                        else:
                            act(g_sig[oc], bank, AF.Sigmoid)
                        cap[f"pj{g % 2}"] = S.cnt["S"]
                        cap[f"pjsrc{m}{oc}"] = S.cnt["S"]
                for m in range(3):
                    for oc in range(4):
                        c = m * 4 + oc
                        bank = ps[2 + (c % 2)]
                        S.wait("tensor", "S", cap[f"pjsrc{m}{oc}"])
                        S.wait("tensor", "S", cap.get(f"cv{c % 2}"))
                        for j in range(KS):
                            mm(bank, conv_diag[m][oc][:, 128 * j:128 * j + 128],
                               convin[m][oc][:, j:j + 512],
                               start=(j == 0), stop=(j == KS - 1), inc=(j == KS - 1))
                        cap[f"cvm{c % 2}"] = S.cnt["T"]
                        S.wait("vector", "T", cap[f"cvm{c % 2}"])
                        vcopy(convin[m][oc][:, 0:3], convin[m][oc][:, 512:515])
                        cap[f"cvt{c % 2}"] = S.cnt["V"]
                        S.wait("scalar", "V", cap[f"cvt{c % 2}"])
                        S.wait("scalar", "T", cap[f"cvm{c % 2}"])
                        act(convin[m][oc][:, 3:515], bank, AF.Silu,
                            bias=cb_f[:, m * 4 + oc:m * 4 + oc + 1])
                        cap[f"cv{c % 2}"] = S.cnt["S"]
                cap["convin_s"] = S.cnt["S"]
                S.wait("sync", "T", cap[f"pjm0"])
                S.wait("sync", "T", cap[f"pjm1"])
                S.wait("sync", "LT")
                dma("sync", mask_su4, masksu4, "Lm")
                dma("sync", mask_iu4, maskiu4, "Lm")
                dma("sync", identb4w, identb4, "Lm")
                cap["mask_ld"] = S.cnt["Lm"]

            def ebcast(ci):
                S.wait("gpsimd", "V", cap.get("eq_free"))
                S.wait("gpsimd", "Lpl", cap.get("lpl_rows"))
                for h in range(H4):
                    c = 4 * ci + h
                    bcast(ebpQ[:, 128 * h:128 * h + 128],
                          lgbt_dram[c:c + 1, :], 128, "GpA")
                    bcast(eapQ[:, 128 * h:128 * h + 128],
                          cumt_dram[c:c + 1, :], 128, "GpA")
                cap[f"gpE{ci}"] = S.cnt["GpA"]

            def heads_cols(ci):
                c2 = ci % 4
                qf = [convin[0][h][:, 3 + 128 * c2:3 + 128 * c2 + 128]
                      for h in range(H4)]
                kf = [convin[1][h][:, 3 + 128 * c2:3 + 128 * c2 + 128]
                      for h in range(H4)]
                vf = [convin[2][h][:, 3 + 128 * c2:3 + 128 * c2 + 128]
                      for h in range(H4)]
                col = [4 * ci + h for h in range(H4)]
                return qf, kf, vf, col

            def t1_part(ci):
                _, kf, _, _ = heads_cols(ci)
                qf = heads_cols(ci)[0]
                S.wait("tensor", "S", cap.get("convin_s"))
                if ci % 4 == 0:
                    S.wait("tensor", "V", cap.get("planes_pft_v"))
                    S.wait("tensor", "S", cap.get("planes_pft_s"))
                for h in range(H4):
                    mm(ps[5][:, 128 * h:128 * h + 128], kf[h], kf[h], start=True,
                       stop=True)
                    mm(pft[:, 128 * h:128 * h + 128], kf[h], qf[h], start=True,
                       stop=True, inc=(h == 3))
                cap[f"t1_{ci}"] = S.cnt["T"]

            def loopA_part(ci, guard_sq6=None):
                _, _, _, col = heads_cols(ci)
                S.wait("vector", "GpA", cap[f"gpE{ci}"])
                for h in range(H4):
                    p = h % 2
                    ebs = ebpQ[:, 128 * h:128 * h + 128]
                    eas = eapQ[:, 128 * h:128 * h + 128]
                    S.wait("vector", "S", cap.get(f"exp_p{p}"))
                    vts(dclA[p], ebs, pl_ncumc[:, col[h]:col[h] + 1],
                        ALU.add, 25.0, ALU.min, inc=False)
                    vts(dclB[p], eas, pl_ncumc2[:, col[h]:col[h] + 1],
                        ALU.add, 25.0, ALU.min)
                    cap[f"clamp_p{p}"] = S.cnt["V"]
                    S.wait("scalar", "V", cap[f"clamp_p{p}"])
                    act(ebs, dclA[p], AF.Exp, inc=False)
                    act(eas, dclB[p], AF.Exp)
                    cap[f"exp_p{p}"] = S.cnt["S"]
                S.wait("vector", "T", cap[f"t1_{ci}"])
                if guard_sq6 is not None:
                    S.wait("vector", "T", guard_sq6)
                S.wait("vector", "S", cap["exp_p0"])
                S.wait("vector", "S", cap["exp_p1"])
                S.wait("vector", "Lm", cap.get("mask_ld"))
                vtt(ALU.mult, tmpAw, ps[5][:, 0:512], ebpQ, inc=False)
                vtt(ALU.mult, tmpBw, pft[:, 0:512], eapQ)
                vsw()
                vtt(ALU.mult, a4w, tmpAw, mask_su4, inc=False)
                vtt(ALU.mult, abarw, tmpBw, mask_iu4)
                cap["eq_free"] = S.cnt["V"]
                cap[f"abuild_{ci}"] = S.cnt["V"]
                if ci % 4 < 3:
                    ebcast(ci + 1)

            def rest_pre(ci):
                _, _, vf, col = heads_cols(ci)
                S.wait("tensor", "V", cap.get("pbt_free"))
                for h in range(H4):
                    trans(pbt[:, 128 * h:128 * h + 128], vf[h], c_identb, inc=False)
                    trans(pbt[:, 512 + 128 * h:512 + 128 * h + 128],
                          convin[1][h][:, 3 + 128 * (ci % 4):3 + 128 * (ci % 4) + 128],
                          c_identb, inc=(h == 3))
                cap["t2a"] = S.cnt["T"]
                S.wait("vector", "T", cap["t2a"])
                for h in range(H4):
                    vts(vtm4[h], pbt[:, 128 * h:128 * h + 128],
                        pl_btm[:, col[h]:col[h] + 1], ALU.mult, inc=False)
                    vts(kg4[h], pbt[:, 512 + 128 * h:512 + 128 * h + 128],
                        pl_grev[:, col[h]:col[h] + 1], ALU.mult, inc=(h == 3))
                cap["v3"] = S.cnt["V"]
                S.wait("tensor", "V", cap["v3"])
                for h in range(H4):
                    trans(pbt[:, 128 * h:128 * h + 128], a4[h], c_identb, inc=(h == 3))
                cap["t2b"] = S.cnt["T"]
                S.wait("vector", "T", cap["t2b"])
                vcopy(q0w, pbt[:, 0:512], inc=False)
                vcopy(pjw, a4w, inc=False)
                vcopy(qjw, pbt[:, 0:512])
                cap["v4"] = S.cnt["V"]
                cap["pbt_free"] = S.cnt["V"]
                prevv = cap["v4"]
                for lev in range(1, 7):
                    S.wait("tensor", "V", prevv)
                    if lev == 1:
                        S.wait("tensor", "V", cap.get("wo_free"))
                    for h in range(H4):
                        mm(ps[2][:, 128 * h:128 * h + 128], qj4[h], pj4[h],
                           start=True, stop=True, inc=(lev == 6 and h == 3))
                        if lev < 6:
                            mm(ps[3][:, 128 * h:128 * h + 128], pj4[h], qj4[h],
                               start=True, stop=True, inc=(h == 3))
                    capt = S.cnt["T"]
                    S.wait("vector", "T", capt)
                    if lev < 6:
                        if lev >= 2:
                            vcopy(qlevW[lev - 2], qjw)
                        vsw()
                        vcopy(pjw, ps[2][:, 0:512], inc=False)
                        vcopy(qjw, ps[3][:, 0:512])
                    else:
                        vcopy(pjw, ps[2][:, 0:512])
                    prevv = S.cnt["V"]
                vsw()
                vtt(ALU.add, ybw[0], identb4w, pjw)
                prevv = S.cnt["V"]
                for i, lev in enumerate(range(5, 0, -1)):
                    S.wait("tensor", "V", prevv)
                    for h in range(H4):
                        qsrc = (qj4[h] if lev == 5
                                else qlevW[lev - 1][:, 128 * h:128 * h + 128])
                        mm(ps[2][:, 128 * h:128 * h + 128], qsrc,
                           ybw[i % 2][:, 128 * h:128 * h + 128],
                           start=True, stop=True, inc=(h == 3))
                    capt = S.cnt["T"]
                    S.wait("vector", "T", capt)
                    vtt(ALU.add, ybw[(i + 1) % 2], ps[2][:, 0:512], ybw[i % 2])
                    prevv = S.cnt["V"]
                S.wait("tensor", "V", prevv)
                for h in range(H4):
                    mm(ps[2][:, 128 * h:128 * h + 128], q04[h],
                       ybw[1][:, 128 * h:128 * h + 128],
                       start=True, stop=True, inc=(h == 3))
                capt = S.cnt["T"]
                S.wait("vector", "T", capt)
                vtt(ALU.subtract, xtw, ybw[1], ps[2][:, 0:512])
                cap[f"xt_{ci}"] = S.cnt["V"]

            def pre_chunk(ci):
                if ci % 4 == 0:
                    ebcast(ci)
                t1_part(ci)
                loopA_part(ci)
                rest_pre(ci)

            def seq_chunk(ci, nxt):
                par = ci % 2
                qf, kf, _, col = heads_cols(ci)
                S.wait("tensor", "V", cap.get(f"sq3_{ci - 1}"))
                S.wait("tensor", "V", cap.get(f"sbf_{ci - 1}"))
                for h in range(H4):
                    mm(ps[0][:, 128 * h:128 * h + 128], kf[h], s_bf[h], start=True,
                       stop=True, inc=(h == 3))
                cap["sq1"] = S.cnt["T"]
                S.wait("vector", "T", cap["sq1"])
                for h in range(H4):
                    vstt(rbf4[h], ps[0][:, 128 * h:128 * h + 128],
                         pl_nexplgb[:, col[h]:col[h] + 1], vtm4[h],
                         ALU.mult, ALU.add)
                cap[f"sq3_{ci}"] = S.cnt["V"]
                S.wait("tensor", "V", cap[f"sq3_{ci}"])
                for h in range(H4):
                    mm(ps[1][:, 128 * h:128 * h + 128], xt4[h], rbf4[h], start=True,
                       stop=True, inc=(h == 3))
                cap["sq4"] = S.cnt["T"]
                S.wait("vector", "T", cap["sq4"])
                vcopy(w4w, ps[1][:, 0:512])
                cap[f"sq5_{ci}"] = S.cnt["V"]
                if nxt is not None:
                    t1_part(nxt)
                S.wait("tensor", "V", cap[f"sq5_{ci}"])
                for h in range(H4):
                    mm(ps[2][:, 128 * h:128 * h + 128], qf[h], s_bf[h], start=True,
                       stop=True)
                    mm(ps[3][:, 128 * h:128 * h + 128], abar4[h], w4[h], start=True,
                       stop=True)
                    mm(ps[4][:, 128 * h:128 * h + 128], kg4[h], w4[h], start=True,
                       stop=True, inc=(h == 3))
                cap[f"sq6_{ci}"] = S.cnt["T"]
                S.wait("vector", "T", cap[f"sq6_{ci}"])
                for h in range(H4):
                    vts(otmp1, ps[2][:, 128 * h:128 * h + 128],
                        pl_gam[:, col[h]:col[h] + 1], ALU.mult)
                    vsw()
                    vtt(ALU.add, orw4[h], ps[3][:, 128 * h:128 * h + 128], otmp1)
                    vstt(s_f32[h][par], s_f32[h][1 - par],
                         pl_gamc[:, col[h]:col[h] + 1],
                         ps[4][:, 128 * h:128 * h + 128], ALU.mult, ALU.add)
                    vsw()
                for h in range(H4):
                    vcopy(s_bf[h], s_f32[h][par], inc=(h == 3))
                cap[f"sbf_{ci}"] = S.cnt["V"]
                S.wait("scalar", "V", cap[f"sbf_{ci}"])
                for h in range(H4):
                    if h:
                        ssw()
                    act(sjunk, orw4[h], AF.Square, accum=stats[:, 16 + h:17 + h])
                cap["sq8"] = S.cnt["S"]
                if nxt is not None:
                    loopA_part(nxt, guard_sq6=cap[f"sq6_{ci}"])
                S.wait("vector", "S", cap["sq8"])
                for h in range(H4):
                    vred(stats[:, 20 + h:21 + h], orw4[h], inc=(h == 3))
                vsw()
                for h in range(H4):
                    vts(stats[:, 24 + h:25 + h], stats[:, 20 + h:21 + h], 1.0 / 128,
                        ALU.mult, inc=False)
                    vts(stats[:, 28 + h:29 + h], stats[:, 16 + h:17 + h], 1.0 / 128,
                        ALU.mult, inc=(h == 3))
                vsw()
                for h in range(H4):
                    vts(stats[:, 32 + h:33 + h], stats[:, 24 + h:25 + h],
                        stats[:, 24 + h:25 + h], ALU.mult, inc=(h == 3))
                vsw()
                for h in range(H4):
                    vtt(ALU.subtract, stats[:, 36 + h:37 + h],
                        stats[:, 28 + h:29 + h], stats[:, 32 + h:33 + h],
                        inc=(h == 3))
                cap["sq9"] = S.cnt["V"]
                S.wait("scalar", "V", cap["sq9"])
                for h in range(H4):
                    act(stats[:, 40 + h:41 + h], stats[:, 36 + h:37 + h], AF.Ln,
                        bias=fconst[:, 0:1], inc=(h == 3))
                ssw()
                for h in range(H4):
                    act(stats[:, 44 + h:45 + h], stats[:, 40 + h:41 + h], AF.Exp,
                        scale=fconst[:, 3:4], inc=(h == 3))
                cap["sq10"] = S.cnt["S"]
                S.wait("vector", "S", cap["sq10"])
                for h in range(H4):
                    vts(n1b1, orw4[h], stats[:, 24 + h:25 + h], ALU.subtract,
                        stats[:, 44 + h:45 + h], ALU.mult)
                    vsw()
                    vtt(ALU.mult, n2b1, n1b1, c_lnw)
                    vsw()
                    vtt(ALU.add, n34[h], n2b1, c_lnb, inc=(h == 3))
                cap["sq11"] = S.cnt["V"]
                S.wait("tensor", "V", cap["sq11"])
                for h in range(H4):
                    trans(pbt[:, 128 * h:128 * h + 128], n34[h], c_identb,
                          inc=(h == 3))
                cap["sq12"] = S.cnt["T"]
                S.wait("vector", "T", cap["sq12"])
                for h in range(H4):
                    vtt(ALU.mult, og_tile[h][:, 128 * (ci % 4):128 * (ci % 4) + 128],
                        pbt[:, 128 * h:128 * h + 128],
                        g_sig[h][:, 128 * (ci % 4):128 * (ci % 4) + 128],
                        inc=(h == 3))
                cap["og"] = S.cnt["V"]
                cap["pbt_free"] = S.cnt["V"]
                if nxt is not None:
                    rest_pre(nxt)

            S.wait("tensor", "Lw")
            S.wait("scalar", "Lw")
            S.wait("gpsimd", "Lw")
            def wo_epi(ti):
                q4 = ti // 2
                thh = ti % 2
                S.wait("tensor", "Cy")
                S.wait("tensor", "S")
                S.wait("tensor", "V", cap["og"])
                for dc in range(16):
                    bank = ps[3 + dc % 2]
                    rot = dc % 2
                    S.wait("tensor", "V", cap.get(f"wo_bk{rot}"))
                    for k in range(4):
                        mm(bank, woj_bf[k][:, 128 * dc:128 * dc + 128], og_tile[k],
                           start=(k == 0), stop=(k == 3), inc=(k == 3))
                    cap[f"wo_mm{rot}"] = S.cnt["T"]
                    S.wait("vector", "T", cap[f"wo_mm{rot}"])
                    S.wait("vector", "Lpl")
                    vcopy(rs_stage[rot], bank)
                    cap[f"wo_bk{rot}"] = S.cnt["V"]
                    S.wait("sync", "V", cap[f"wo_bk{rot}"])
                    dma("sync", rs_in[2048 * q4 + 128 * dc:2048 * q4 + 128 * dc + 128,
                                      512 * thh:512 * thh + 512], rs_stage[rot],
                        "Lpl")
                    if ti == 0 and dc == 0 and "t_rstg" in tap_aps:
                        dma("sync", tap_aps["t_rstg"], rs_stage[rot], "LT")
                cap["wo_free"] = S.cnt["V"]

            xn_fm_3d = xn_fm.rearrange("p (c f) -> p c f", f=512)
            for ti in range(NT):
                for s4 in range(4):
                    t0 = ti * TILE + s4 * 128
                    ln_subtile(xtm[t0:t0 + 128, :])
                    xn_transpose(xn_fm_3d[:, :, 128 * s4:128 * s4 + 128])
                if ti == 0 and "t_xn" in tap_aps:
                    S.wait("sync", "V")
                    dma("sync", tap_aps["t_xn"].rearrange("(c p) t -> p c t", p=128),
                        xn_fm_3d, "LT")
                if ti == 0:
                    ckpt("ln")
                if ti == 0:
                    S.wait("tensor", "Cw")
                    S.wait("scalar", "Cw")
                    S.wait("gpsimd", "Cw")
                planes_for_tile(ti)
                if ti == 0:
                    ckpt("planes")
                proj_and_conv(ti)
                if ti == 0:
                    ckpt("proj")
                if ti == 0:
                    for nm, tiles in (("t_q", convin[0]), ("t_k", convin[1]),
                                      ("t_v", convin[2]), ("t_g", g_sig)):
                        if nm in tap_aps:
                            S.wait("sync", "S")
                            w = tap_aps[nm].shape[1]
                            for oc in range(4):
                                dma("sync", tap_aps[nm][128 * oc:128 * oc + 128, :],
                                    tiles[oc][:, 0:w], "LT")
                    if "t_ab" in tap_aps:
                        S.wait("sync", "S")
                        dma("sync", tap_aps["t_ab"][0:8, :], ab_sig, "LT")
                        dma("sync", tap_aps["t_ab"][8:16, :], ab_ln, "LT")
                    for nm in ("t_q", "t_k", "t_v", "t_g", "t_ab"):
                        if nm in tap_aps:
                            S.wait("scalar", "LT")
                            S.wait("vector", "LT")
                            break
                if ti > 0:
                    wo_epi(ti - 1)
                pre_chunk(ti * 4)
                for c2 in range(4):
                    ci = ti * 4 + c2
                    seq_chunk(ci, ci + 1 if c2 < 3 else None)
                    if ci == 0:
                        ckpt("seq0")
                if ti == NT - 1:
                    wo_epi(ti)
                if ti == 0 and "t_og" in tap_aps:
                    S.wait("sync", "V")
                    for hh in range(4):
                        dma("sync", tap_aps["t_og"][128 * hh:128 * hh + 128, :],
                            og_tile[hh], "LT")
                    S.wait("vector", "LT")
                    S.wait("tensor", "LT")

            for nm, pl in (("t_cum", pl_cum), ("t_btm", pl_btm), ("t_grev", pl_grev),
                           ("t_gam", pl_gam), ("t_gamc", pl_gamc),
                           ("t_nexplgb", pl_nexplgb)):
                if nm in tap_aps:
                    S.wait("sync", "V")
                    S.wait("sync", "S")
                    S.wait("sync", "Gp")
                    dma("sync", tap_aps[nm], pl, "LT")
            if "t_lgbt" in tap_aps:
                S.wait("sync", "Lpl")
                dma("sync", tap_aps["t_lgbt"], lgbt_dram, "LT")
                dma("sync", tap_aps["t_cumt"], cumt_dram, "LT")
            if "t_rsin" in tap_aps:
                S.wait("sync", "Lpl")
                dma("sync", tap_aps["t_rsin"], rs_in, "LT")
                S.wait("gpsimd", "LT")
            ckpt("A")
            S.wait("gpsimd", "Lpl")
            S.emit("gpsimd", lambda e: e.collective_compute(
                "ReduceScatter", ALU.add,
                replica_groups=[[0, 1, 2, 3], [4, 5, 6, 7]],
                ins=[rs_in.opt()], outs=[rs_out.opt()]), [("CC", 1)])

            S.barrier()
            abf.pop()
            abf.push()
            xnb = [abf.alloc(1024, name=f"xnb{k}") for k in range(16)]
            abf.push()
            x_tok = abf.alloc(2048, F32, name="x_tokB")
            sq_scr = abf.alloc(2048, F32, name="sq_scrB")
            xn_tok_bf = abf.alloc(2048, name="xn_tokB")

            for s8 in range(8):
                ln_subtile(xtmj[128 * s8:128 * s8 + 128, :])
                for half in range(2):
                    S.wait("tensor", "V")
                    for kk in range(8):
                        k = half * 8 + kk
                        trans(pbt[:, 128 * kk:128 * kk + 128],
                              xn_tok_bf[:, 128 * k:128 * k + 128], c_identb, inc=(kk == 7))
                    S.wait("vector", "T")
                    for kk in range(8):
                        k = half * 8 + kk
                        vcopy(xnb[k][:, 128 * s8:128 * s8 + 128],
                              pbt[:, 128 * kk:128 * kk + 128], inc=(kk == 7))
                    cap["pbt_free"] = S.cnt["V"]
            S.barrier(engines=("sync", "vector", "tensor"), sems=("V", "T", "Lx"))
            abf.pop()
            wupb = [abf.alloc(1024, name=f"wupb{k}") for k in range(16)]
            wdnb = [abf.alloc(2048, name=f"wdnb{k}") for k in range(8)]
            x2 = [abf.alloc(1024, name=f"x2_{k}") for k in range(8)]
            w8u_rot = [abf.alloc(1024, I8, name=f"w8u{i}") for i in range(4)]
            w8d_rot = [abf.alloc(2048, I8, name=f"w8d{i}") for i in range(4)]
            scld_rot = [abf.alloc(2048, name=f"scld{i}") for i in range(4)]
            supt_sb = abf.alloc(HF, name="supt_sb")[0:32, :]
            selu_sb = abf.alloc(2048, name="selu_sb")[0:32, :]
            wtmp = abf.alloc(2048, name="wtmpB")
            y2st = [abf.alloc(512, F32, name=f"y2st{i}") for i in range(2)]

            ckpt("bpro")
            dma("sync", supt_sb, supt_bf, "Lw")
            dma("sync", selu_sb, sel_up, "Lw")
            cap["usc_ld"] = S.cnt["Lw"]
            S.wait("tensor", "V")
            for k in range(16):
                for th in range(2):
                    mm(ps[th][0:16, :], ulab[:, 16 * k:16 * k + 16],
                       xnb[k][:, 512 * th:512 * th + 512],
                       start=(k == 0), stop=(k == 15), inc=(k == 15 and th == 1))
            S.wait("vector", "T")
            vcopy(l1_bf[:, 0:512], ps[0][0:16, :], inc=False)
            vcopy(l1_bf[:, 512:1024], ps[1][0:16, :], inc=False)
            vmemset(l2_f, 0.0)

            ckpt("l1")
            for hg in range(8):
                if hg == 1:
                    ckpt("hg0")
                S.wait("gpsimd", "T", cap.get("ulbb_free"))
                dma("gpsimd", ulbb, ulb[:, 1024 * hg:1024 * hg + 1024], "Cw")
                cap["ulbb_ld"] = S.cnt["Cw"]
                S.wait("vector", "T", cap.get("upmm_last"))
                for k in range(16):
                    rot = k % 4
                    q = ("QuA", "QuB", "QuC", "QuD")[rot]
                    S.wait("sync", "V", cap.get(f"uq_free{rot}"))
                    dma("sync", w8u_rot[rot],
                        wup8[128 * k:128 * k + 128, 1024 * hg:1024 * hg + 1024], q)
                    cap[f"uq{rot}"] = S.cnt[q]
                    bkA, bkB = (ps[4], ps[5]) if k % 2 == 0 else (ps[2], ps[3])
                    S.wait("tensor", "Lw", cap["usc_ld"])
                    S.wait("tensor", "V", cap.get(f"usc_free{k % 2}"))
                    if k < 2:
                        for b in range(4):
                            S.wait("tensor", "V", cap.get(f"dnevac{b}"))
                    for half in range(2):
                        bk = bkA if half == 0 else bkB
                        mm(bk[:, 0:512], selu_sb[:, 128 * k:128 * k + 128],
                           supt_sb[:, 1024 * hg + 512 * half:
                                   1024 * hg + 512 * half + 512],
                           start=True, stop=True, inc=(half == 1))
                    cap[f"uscmm{k % 2}"] = S.cnt["T"]
                    S.wait("vector", q, cap[f"uq{rot}"])
                    S.wait("vector", "T", cap[f"uscmm{k % 2}"])
                    vstt(wupb[k][:, 0:512], w8u_rot[rot][:, 0:512], -7.5,
                         bkA[:, 0:512], ALU.add, ALU.mult, inc=False)
                    vstt(wupb[k][:, 512:1024], w8u_rot[rot][:, 512:1024], -7.5,
                         bkB[:, 0:512], ALU.add, ALU.mult)
                    cap[f"uq_free{rot}"] = S.cnt["V"]
                    cap[f"usc_free{k % 2}"] = S.cnt["V"]
                    cap[f"updqk{k}"] = S.cnt["V"]
                cap["updq_ps"] = S.cnt["V"]
                S.wait("tensor", "Cw", cap["ulbb_ld"])
                g = 0
                for hc in range(8):
                    for th in range(2):
                        bank = ps[g % 2]
                        S.wait("tensor", "S", cap.get(f"upevac{g % 2}"))
                        for k in range(16):
                            if g == 0:
                                S.wait("tensor", "V", cap[f"updqk{k}"])
                            mm(bank, wupb[k][:, 128 * hc:128 * hc + 128],
                               xnb[k][:, 512 * th:512 * th + 512],
                               start=(k == 0), stop=False, inc=False)
                        mm(bank, ulbb[:, 128 * hc:128 * hc + 128],
                           l1_bf[:, 512 * th:512 * th + 512],
                           start=False, stop=True, inc=True)
                        cap[f"upmm{g % 2}"] = S.cnt["T"]
                        S.wait("scalar", "T", cap[f"upmm{g % 2}"])
                        if g == 0:
                            S.wait("scalar", "T", cap.get("dnmm_last"))
                        act(x2[hc][:, 512 * th:512 * th + 512], bank, AF.Relu,
                            bias=bupv[:, hg * 8 + hc:hg * 8 + hc + 1])
                        cap[f"upevac{g % 2}"] = S.cnt["S"]
                        g += 1
                cap["upmm_last"] = S.cnt["T"]
                cap["ulbb_free"] = S.cnt["T"]
                S.wait("tensor", "S", cap["upevac0"])
                S.wait("tensor", "S", cap["upevac1"])
                S.wait("tensor", "V", cap.get("dnevac0"))
                S.wait("tensor", "V", cap.get("dnevac1"))
                S.wait("tensor", "V", cap["updq_ps"])
                for th in range(2):
                    for hc in range(8):
                        mm(ps[2][0:16, :] if th == 0 else ps[3][0:16, :],
                           dlab[:, 16 * (8 * hg + hc):16 * (8 * hg + hc) + 16],
                           x2[hc][:, 512 * th:512 * th + 512],
                           start=(hc == 0), stop=(hc == 7), inc=(hc == 7))
                cap["l2mm"] = S.cnt["T"]
                S.wait("vector", "T", cap["l2mm"])
                vtt(ALU.add, l2_tmp[:, 0:512], l2_f[:, 0:512], ps[2][0:16, :],
                    inc=False)
                vtt(ALU.add, l2_tmp[:, 512:1024], l2_f[:, 512:1024], ps[3][0:16, :])
                vsw()
                vcopy(l2_f, l2_tmp)
                cap["dnevac0"] = S.cnt["V"]
                cap["dnevac1"] = S.cnt["V"]
                S.wait("vector", "T", cap.get("dnmm_last"))
                for hc in range(8):
                    rot = hc % 4
                    q = ("QdA", "QdB", "QdC", "QdD")[rot]
                    hrow = 1024 * hg + 128 * hc
                    S.wait("sync", "V", cap.get(f"dq_free{rot}"))
                    dma("sync", w8d_rot[rot], wdn8[hrow:hrow + 128, :], q)
                    for pp in range(2):
                        hbcast(scld_rot[rot][64 * pp:64 * pp + 64, :],
                               sdnt_bf[2 * (8 * hg + hc) + pp:
                                       2 * (8 * hg + hc) + pp + 1, :], 64, q)
                    cap[f"dq{rot}"] = S.cnt[q]
                    S.wait("vector", q, cap[f"dq{rot}"])
                    vstt(wdnb[hc], w8d_rot[rot], -7.5, scld_rot[rot],
                         ALU.add, ALU.mult)
                    cap[f"dq_free{rot}"] = S.cnt["V"]
                    cap[f"dndqk{hc}"] = S.cnt["V"]
                g = 0
                for dc in range(16):
                    for th in range(2):
                        bank = ps[2 + g % 4]
                        S.wait("tensor", "V", cap.get(f"dnevac{g % 4}"))
                        for hc in range(8):
                            if g == 0:
                                S.wait("tensor", "V", cap[f"dndqk{hc}"])
                            mm(bank, wdnb[hc][:, 128 * dc:128 * dc + 128],
                               x2[hc][:, 512 * th:512 * th + 512],
                               start=(hc == 0), stop=(hc == 7), inc=(hc == 7))
                        cap[f"dnmm{g % 4}"] = S.cnt["T"]
                        rot = g % 2
                        gy = "GyA" if rot == 0 else "GyB"
                        S.wait("vector", "T", cap[f"dnmm{g % 4}"])
                        S.wait("vector", gy, cap.get(f"y2st{rot}"))
                        vcopy(y2st[rot], bank)
                        cap[f"dnevac{g % 4}"] = S.cnt["V"]
                        S.wait("gpsimd", "V", cap[f"dnevac{g % 4}"])
                        if hg == 0:
                            dma("gpsimd",
                                y2_dram[128 * dc:128 * dc + 128,
                                        512 * th:512 * th + 512],
                                y2st[rot], gy)
                        else:
                            dma_acc(
                                y2_dram[128 * dc:128 * dc + 128,
                                        512 * th:512 * th + 512],
                                y2st[rot], gy)
                        cap[f"y2st{rot}"] = S.cnt[gy]
                        g += 1
                cap["dnmm_last"] = S.cnt["T"]

            ckpt("bdone")
            S.wait("vector", "T")
            vsw()
            vcopy(l2_bf, l2_f)
            if "t_y2" in tap_aps:
                S.wait("sync", "GyA")
                S.wait("sync", "GyB")
                dma("sync", tap_aps["t_y2"], y2_dram, "LT")
            if "t_l1" in tap_aps:
                S.wait("sync", "V")
                dma("sync", tap_aps["t_l1"], l1_bf, "LT")
            if "t_l2f" in tap_aps:
                S.wait("sync", "V")
                dma("sync", tap_aps["t_l2f"], l2_f, "LT")
            if "t_x2" in tap_aps:
                S.wait("sync", "S")
                for hc in range(8):
                    dma("sync", tap_aps["t_x2"][128 * hc:128 * hc + 128, :], x2[hc], "LT")
            S.barrier()
            abf.pop()
            xr_f = [abf.alloc(512, F32, name=f"xr_f{i}") for i in range(2)]
            outf = [abf.alloc(512, F32, name=f"outf{i}") for i in range(2)]
            t1_f = [abf.alloc(512, F32, name=f"t1_f{i}") for i in range(2)]
            t2_f = [abf.alloc(512, F32, name=f"t2_f{i}") for i in range(2)]
            y2r = [abf.alloc(512, F32, name=f"y2r{i}") for i in range(2)]
            at_bf = [abf.alloc(512, name=f"atb{i}") for i in range(2)]

            ckpt("wo")
            S.wait("sync", "CC", 1)
            if "t_ofull" in tap_aps:
                dma("sync", tap_aps["t_ofull"], rs_out, "LT")

            ckpt("og")
            S.wait("sync", "GyA")
            S.wait("sync", "GyB")
            cap["cv0"] = None
            for dc in range(16):
                for th in range(2):
                    g = dc * 2 + th
                    rot = g % 2
                    bank = ps[rot]
                    lc = "LcA" if rot == 0 else "LcB"
                    st = "STA" if rot == 0 else "STB"
                    S.wait("sync", "V", cap.get(f"c_used{rot}"))
                    dma("sync", xr_f[rot], xfmj[128 * dc:128 * dc + 128,
                                               512 * th:512 * th + 512], lc)
                    dma("sync", y2r[rot],
                        y2_dram[128 * dc:128 * dc + 128, 512 * th:512 * th + 512],
                        lc)
                    dma("sync", at_bf[rot],
                        rs_out[128 * dc:128 * dc + 128, 512 * th:512 * th + 512],
                        lc)
                    cap[f"c_ld{rot}"] = S.cnt[lc]
                    S.wait("tensor", "V", cap.get(f"c_bank{rot}"))
                    if g == 0:
                        S.wait("tensor", "V")
                    mm(bank, dlb_bf[:, 128 * dc:128 * dc + 128],
                       l2_bf[:, 512 * th:512 * th + 512], start=True, stop=True,
                       inc=True)
                    cap[f"c_mm{rot}"] = S.cnt["T"]
                    S.wait("vector", "T", cap[f"c_mm{rot}"])
                    S.wait("vector", lc, cap[f"c_ld{rot}"])
                    vtt(ALU.add, t1_f[rot], bank, y2r[rot])
                    cap[f"c_bank{rot}"] = S.cnt["V"]
                    vsw()
                    vtt(ALU.add, t2_f[rot], t1_f[rot], at_bf[rot])
                    vsw()
                    S.wait("vector", st, cap.get(f"c_st{rot}"))
                    vstt(outf[rot], xr_f[rot], bdnv[:, dc:dc + 1], t2_f[rot],
                         ALU.add, ALU.add)
                    cap[f"c_out{rot}"] = S.cnt["V"]
                    cap[f"c_used{rot}"] = S.cnt["V"]
                    S.wait("sync", "V", cap[f"c_out{rot}"])
                    dma("sync", out_d[128 * dc:128 * dc + 128,
                                      512 * th:512 * th + 512], outf[rot], st)
                    cap[f"c_st{rot}"] = S.cnt[st]
            for _s in ("Lw", "Cw", "Lx", "Lm", "Lpl", "Lu8", "Ld8", "Lc", "LT", "ST",
                       "QuA", "QuB", "QuC", "QuD", "QdA", "QdB", "QdC", "QdD",
                       "LcA", "LcB", "STA", "STB"):
                S.wait("sync", _s)
            for _s in ("Gs", "Gp", "GpA", "GpB", "Gy", "GyA", "GyB", "CC", "ST",
                       "Cy"):
                S.wait("gpsimd", _s)

            ckpt("full")
        except _StopSched:
            pass
        with nc.Block() as block:
            block.sync(lambda e: [t(e) for t in S.q["sync"]])
            block.scalar(lambda e: [t(e) for t in S.q["scalar"]])
            block.vector(lambda e: [t(e) for t in S.q["vector"]])
            block.tensor(lambda e: [t(e) for t in S.q["tensor"]])
            block.gpsimd(lambda e: [t(e) for t in S.q["gpsimd"]])

    return nc



def _mk_sel_up():
    sel = np.zeros((32, 2048), np.float32)
    for k in range(16):
        sel[2 * k, 128 * k:128 * k + 64] = 1.0
        sel[2 * k + 1, 128 * k + 64:128 * k + 128] = 1.0
    import ml_dtypes
    return np.ascontiguousarray(sel).astype(ml_dtypes.bfloat16)


def _prep_core_inputs(inputs):
    bf16 = ml_dtypes.bfloat16
    x = np.asarray(inputs["L0_attn_x"], dtype=np.float32)
    Wq = np.asarray(inputs["L0_attn_Wq"], np.float32)
    Wk = np.asarray(inputs["L0_attn_Wk"], np.float32)
    Wv = np.asarray(inputs["L0_attn_Wv"], np.float32)
    Wg = np.asarray(inputs["L0_attn_Wg"], np.float32)
    Wa = np.asarray(inputs["L0_attn_Wa"], np.float32)
    Wb = np.asarray(inputs["L0_attn_Wb"], np.float32)
    ba = np.asarray(inputs["L0_attn_ba"], np.float32)
    bb = np.asarray(inputs["L0_attn_bb"], np.float32)
    Wo = np.ascontiguousarray(np.asarray(inputs["L0_attn_Wo"], np.float32))
    cqw = np.asarray(inputs["L0_attn_conv_q_w"], np.float32)[:, 0, :]
    ckw = np.asarray(inputs["L0_attn_conv_k_w"], np.float32)[:, 0, :]
    cvw = np.asarray(inputs["L0_attn_conv_v_w"], np.float32)[:, 0, :]
    cqb = np.asarray(inputs["L0_attn_conv_q_b"], np.float32)
    ckb = np.asarray(inputs["L0_attn_conv_k_b"], np.float32)
    cvb = np.asarray(inputs["L0_attn_conv_v_b"], np.float32)
    lnw = np.asarray(inputs["L0_attn_ln_w"], np.float32)
    lnb = np.asarray(inputs["L0_attn_ln_b"], np.float32)
    upq = np.asarray(inputs["L0_ffn_w_up_q"], np.int32)
    ups = np.asarray(inputs["L0_ffn_w_up_scale"], np.float32)
    bup = np.asarray(inputs["L0_ffn_b_up"], np.float32)
    ula_ = np.asarray(inputs["L0_ffn_w_up_lora_a"], np.float32)
    ulb_ = np.asarray(inputs["L0_ffn_w_up_lora_b"], np.float32)
    dnq = np.asarray(inputs["L0_ffn_w_down_q"], np.int32)
    dns = np.asarray(inputs["L0_ffn_w_down_scale"], np.float32)
    bdn = np.asarray(inputs["L0_ffn_b_down"], np.float32)
    dla_ = np.asarray(inputs["L0_ffn_w_down_lora_a"], np.float32)
    dlb_ = np.asarray(inputs["L0_ffn_w_down_lora_b"], np.float32)

    upq_t = np.ascontiguousarray(upq.T)
    wup8 = np.ascontiguousarray(upq_t.view(np.int8)[:, ::4])
    dnq_t = np.ascontiguousarray(dnq.T)
    wdn8 = np.ascontiguousarray(dnq_t.view(np.int8)[:, ::4])

    shared = dict(
        lnwb=np.ascontiguousarray(np.broadcast_to(lnw[None, :], (128, 128))),
        lnbb=np.ascontiguousarray(np.broadcast_to(lnb[None, :], (128, 128))),
        wup8=wup8, wdn8=wdn8,
        supt=np.ascontiguousarray(ups.T),
        sdnt=np.ascontiguousarray(dns.T),
        bupt=np.ascontiguousarray(bup.reshape(64, 128).T),
        bdnt=np.ascontiguousarray(bdn.reshape(16, 128).T),
        ula=ula_, ulb=ulb_, dla=dla_, dlb=dlb_,
        identb=np.eye(128, dtype=np.float32).astype(bf16),
        identf=np.eye(128, dtype=np.float32),
        tuones=np.ascontiguousarray(np.triu(np.ones((128, 128), np.float32))),
        masksu4=np.ascontiguousarray(
            np.tile(np.triu(np.ones((128, 128), np.float32), 1), (1, 4))).astype(bf16),
        maskiu4=np.ascontiguousarray(
            np.tile(np.triu(np.ones((128, 128), np.float32)), (1, 4))).astype(bf16),
        identb4=np.ascontiguousarray(
            np.tile(np.eye(128, dtype=np.float32), (1, 4))).astype(bf16),
        sel_up=_mk_sel_up(),
    )
    xfm = [np.ascontiguousarray(x[g].T) for g in range(2)]

    def per_core(core):
        g, j = core // 4, core % 4
        hsl = slice(512 * j, 512 * j + 512)
        d = dict(shared)
        d.update(
            xtm=x[g],
            xtmj=np.ascontiguousarray(x[g][TOK * j:TOK * j + TOK, :]),
            xfmj=np.ascontiguousarray(xfm[g][:, TOK * j:TOK * j + TOK]),
            wq=np.ascontiguousarray(Wq[:, hsl]),
            wk=np.ascontiguousarray(Wk[:, hsl]),
            wv=np.ascontiguousarray(Wv[:, hsl]),
            wg=np.ascontiguousarray(Wg[:, hsl]),
            wab=np.ascontiguousarray(np.concatenate(
                [Wa[:, 4 * j:4 * j + 4], Wb[:, 4 * j:4 * j + 4]], axis=1)),
            abb=np.ascontiguousarray(np.concatenate(
                [ba[4 * j:4 * j + 4], bb[4 * j:4 * j + 4]]).reshape(8, 1)),
            convw=np.ascontiguousarray(np.concatenate(
                [w[hsl].reshape(4, 128, KS).transpose(1, 0, 2).reshape(128, 4 * KS)
                 for w in (cqw, ckw, cvw)], axis=1)),
            convb=np.ascontiguousarray(np.concatenate(
                [b[hsl].reshape(4, 128).T for b in (cqb, ckb, cvb)], axis=1)),
            woj=np.ascontiguousarray(Wo[hsl, :]),
        )
        return d

    return per_core


def _get_nc():
    key = ("v1", tuple(n for n, _, _ in _TAPS))
    if key not in _NC_CACHE:
        _NC_CACHE[key] = build_nc(_TAPS)
    return _NC_CACHE[key]


def kernel(**inputs):
    per_core = _prep_core_inputs(inputs)
    nc = _get_nc()
    in_maps = [per_core(c) for c in range(8)]
    res = run_bass_kernel_spmd(nc, in_maps, core_ids=list(range(8)))
    out = np.empty((2, T, HID), dtype=np.float32)
    for core in range(8):
        g, j = core // 4, core % 4
        out[g, TOK * j:TOK * j + TOK, :] = np.asarray(res.results[core]["out"]).T
    if _TAPS:
        kernel._tap_results = [
            {n: np.asarray(res.results[c][n]) for n, _, _ in _TAPS} for c in range(8)]
    return out


# version 14
# speedup vs baseline: 1.0155x; 1.0155x over previous
import math

import numpy as np
import ml_dtypes

import concourse.bass as bass
import concourse.mybir as mybir
from concourse.bass_utils import run_bass_kernel_spmd

F32 = mybir.dt.float32
BF16 = mybir.dt.bfloat16
I8 = mybir.dt.int8
I32 = mybir.dt.int32
AF = mybir.ActivationFunctionType
ALU = mybir.AluOpType
AX = mybir.AxisListType

HID = 2048
H4 = 4
T = 4096
NT = 8
TILE = 512
KS = 4
HF = 8192
TOK = 1024
LN128 = math.log(128.0)

ENGS = ("sync", "scalar", "vector", "tensor", "gpsimd")

_TAPS = []
_STOP = None
_NC_CACHE = {}


class _StopSched(Exception):
    pass


class Sched:
    def __init__(self, nc):
        self.nc = nc
        self.q = {e: [] for e in ENGS}
        self.cnt = {}
        self.sems = {}

    def emit(self, eng, build, incs=()):
        incs = tuple(incs)
        for name, amt in incs:
            self.cnt[name] = self.cnt.get(name, 0) + amt

        def thunk(e, _b=build, _i=incs, _s=self.sems):
            ins = _b(e)
            for name, amt in _i:
                ins = ins.then_inc(_s[name], amt)

        self.q[eng].append(thunk)

    def wait(self, eng, semname, val=None):
        v = self.cnt.get(semname, 0) if val is None else val
        if v <= 0:
            return

        def thunk(e, _n=semname, _v=v, _s=self.sems):
            e.wait_ge(_s[_n], _v)

        self.q[eng].append(thunk)

    def barrier(self, engines=ENGS,
                sems=("Lw", "Cw", "Cy", "Lx", "Lm", "Lpl", "Lu8", "Ld8", "Lc", "LT", "Gs", "Gp", "GpA", "GpB", "Gy", "GyA", "GyB", "QuA", "QuB", "QuC", "QuD", "QdA", "QdB", "QdC", "QdD", "LcA", "LcB", "STA", "STB",
                      "T", "V", "S", "ST")):
        for eng in engines:
            for s in sems:
                self.wait(eng, s)


class Arena:

    def __init__(self, ap, cols):
        self.ap = ap
        self.cols = cols
        self.off = 0
        self.marks = []
        self.peak = 0

    def alloc(self, n, dtype=BF16, name=""):
        if dtype == I8:
            cols = (n + 1) // 2
        elif dtype == F32:
            cols = 2 * n
        else:
            cols = n
        self.off = (self.off + 1) // 2 * 2
        assert self.off + cols <= self.cols, (
            f"arena overflow: {name} needs {cols} cols at {self.off}, cap {self.cols}")
        sl = self.ap[:, self.off:self.off + cols]
        self.off += cols
        self.peak = max(self.peak, self.off)
        if dtype != BF16:
            sl = sl.bitcast(dtype)
        return sl

    def view(self, off, n, dtype=BF16):
        cols = 2 * n if dtype == F32 else n
        sl = self.ap[:, off:off + cols]
        if dtype != BF16:
            sl = sl.bitcast(dtype)
        return sl

    def push(self):
        self.marks.append(self.off)

    def pop(self):
        self.off = self.marks.pop()


def build_nc(taps=()):
    nc = bass.Bass("TRN2", target_bir_lowering=False, debug=False)
    tap_aps = {}
    for name, shape, dtp in taps:
        tap_aps[name] = nc.dram_tensor(name, shape, dtp, kind="ExternalOutput").ap()

    dt = nc.dram_tensor
    xtm = dt("xtm", [T, HID], F32, kind="ExternalInput").ap()
    xtmj = dt("xtmj", [TOK, HID], F32, kind="ExternalInput").ap()
    xfmj = dt("xfmj", [HID, TOK], F32, kind="ExternalInput").ap()
    wq = dt("wq", [HID, 512], F32, kind="ExternalInput").ap()
    wk = dt("wk", [HID, 512], F32, kind="ExternalInput").ap()
    wv = dt("wv", [HID, 512], F32, kind="ExternalInput").ap()
    wg = dt("wg", [HID, 512], F32, kind="ExternalInput").ap()
    wab = dt("wab", [HID, 8], F32, kind="ExternalInput").ap()
    abb = dt("abb", [8, 1], F32, kind="ExternalInput").ap()
    convw = dt("convw", [128, 3 * 4 * KS], F32, kind="ExternalInput").ap()
    convb = dt("convb", [128, 3 * 4], F32, kind="ExternalInput").ap()
    lnwb = dt("lnwb", [128, 128], F32, kind="ExternalInput").ap()
    lnbb = dt("lnbb", [128, 128], F32, kind="ExternalInput").ap()
    woj = dt("woj", [512, HID], F32, kind="ExternalInput").ap()
    wup8 = dt("wup8", [HID, HF], I8, kind="ExternalInput").ap()
    wdn8 = dt("wdn8", [HF, HID], I8, kind="ExternalInput").ap()
    supt = dt("supt", [32, HF], F32, kind="ExternalInput").ap()
    sdnt = dt("sdnt", [128, HID], F32, kind="ExternalInput").ap()
    supt_bf = dt("supt_bf", [32, HF], BF16).ap()
    sel_up = dt("sel_up", [32, 2048], BF16, kind="ExternalInput").ap()
    sdnt_bf = dt("sdnt_bf", [128, HID], BF16).ap()
    bupt = dt("bupt", [128, 64], F32, kind="ExternalInput").ap()
    bdnt = dt("bdnt", [128, 16], F32, kind="ExternalInput").ap()
    ula = dt("ula", [HID, 16], F32, kind="ExternalInput").ap()
    ulb = dt("ulb", [16, HF], F32, kind="ExternalInput").ap()
    dla = dt("dla", [HF, 16], F32, kind="ExternalInput").ap()
    dlb = dt("dlb", [16, HID], F32, kind="ExternalInput").ap()
    identb = dt("identb", [128, 128], BF16, kind="ExternalInput").ap()
    identf = dt("identf", [128, 128], F32, kind="ExternalInput").ap()
    tuones = dt("tuones", [128, 128], F32, kind="ExternalInput").ap()
    masksu4 = dt("masksu4", [128, 512], BF16, kind="ExternalInput").ap()
    maskiu4 = dt("maskiu4", [128, 512], BF16, kind="ExternalInput").ap()
    identb4 = dt("identb4", [128, 512], BF16, kind="ExternalInput").ap()

    out_d = dt("out", [HID, TOK], F32, kind="ExternalOutput").ap()

    lgbt_dram = dt("lgbt_dram", [128, 128], F32).ap()
    cumt_dram = dt("cumt_dram", [128, 128], F32).ap()
    clast_dram = dt("clast_dram", [1, 128], F32).ap()
    rs_in = dt("rs_in", [4 * HID, TOK], BF16).ap()
    rs_out = dt("rs_out", [HID, TOK], BF16).ap()
    y2_dram = dt("y2_dram", [HID, TOK], F32).ap()

    def ckpt(name):
        if _STOP == name:
            raise _StopSched

    from contextlib import ExitStack
    with ExitStack() as cx:
        ACOLS = 88320
        abf = Arena(cx.enter_context(nc.sbuf_tensor("abf", [128, ACOLS], BF16))[:, :],
                    ACOLS)

        def sb(name, shape, dtp):
            return cx.enter_context(nc.sbuf_tensor(name, shape, dtp))[
                tuple(slice(None) for _ in shape)]

        c_identb = sb("c_identb", [128, 128], BF16)
        c_identf = sb("c_identf", [128, 128], F32)
        c_tuones = sb("c_tuones", [128, 128], F32)
        c_lnw = sb("c_lnw", [128, 128], F32)
        c_lnb = sb("c_lnb", [128, 128], F32)
        pl_cum = sb("pl_cum", [128, 128], F32)
        pl_ncumc = sb("pl_ncumc", [128, 128], F32)
        pl_ncumc2 = sb("pl_ncumc2", [128, 128], F32)
        pl_gam = sb("pl_gam", [128, 128], F32)
        pl_lgb = sb("pl_lgb", [128, 128], F32)
        pl_nexplgb = sb("pl_nexplgb", [128, 128], F32)
        pl_btm = sb("pl_btm", [128, 128], F32)
        pl_clb = sb("pl_clb", [128, 128], F32)
        pl_grev = sb("pl_grev", [128, 128], F32)
        pl_gamc = sb("pl_gamc", [128, 128], F32)
        ab_sig = sb("ab_sig", [8, 512], F32)
        ab_ln = sb("ab_ln", [8, 512], F32)
        la_tile = sb("la_tile", [128, 16], F32)
        pl_scr = sb("pl_scr", [128, 16], F32)
        lb_tile = sb("lb_tile", [128, 16], F32)
        abbv = sb("abbv", [8, 1], F32)
        cw_f = sb("cw_f", [128, 3 * 4 * KS], F32)
        cb_f = sb("cb_f", [128, 3 * 4], F32)
        plstage = sb("plstage", [16, 128], F32)
        stats = sb("stats", [128, 48], F32)
        dlb_bf = sb("dlb_bf", [16, HID], BF16)
        l1_bf = sb("l1_bf", [16, TOK], BF16)
        l2_f = sb("l2_f", [16, TOK], F32)
        l2_tmp = sb("l2_tmp", [16, TOK], F32)
        l2_bf = sb("l2_bf", [16, TOK], BF16)
        ulab = sb("ulab", [128, 16 * 16], BF16)
        ulbb = sb("ulbb", [16, 1024], BF16)
        dlab = sb("dlab", [128, 16 * 64], BF16)
        bupv = sb("bupv", [128, 64], F32)
        bdnv = sb("bdnv", [128, 16], F32)
        fconst = sb("fconst", [128, 4], F32)

        ps = [cx.enter_context(nc.psum_tensor(f"ps{i}", [128, 512], F32))[:, :]
              for i in range(6)]
        pft = cx.enter_context(nc.psum_tensor("pft", [128, 512], F32))[:, :]
        pbt = cx.enter_context(nc.psum_tensor("pbt", [128, 1024], BF16))[:, :]

        S = Sched(nc)
        for s in ("Lw", "Cw", "Cy", "Lx", "Lm", "Lpl", "Lu8", "Ld8", "Lc", "LT", "Gs", "Gp", "GpA", "GpB", "Gy", "GyA", "GyB", "QuA", "QuB", "QuC", "QuD", "QdA", "QdB", "QdC", "QdD", "LcA", "LcB", "STA", "STB",
                  "T", "V", "S", "CC", "ST"):
            S.cnt[s] = 0
            S.sems[s] = cx.enter_context(nc.semaphore(s))

        def dma(eng, out, in_, sem):
            S.emit(eng, lambda e, _o=out, _i=in_: e.dma_start(out=_o, in_=_i),
                   [(sem, 16)])

        def dma_acc(out, in_, sem):
            S.emit("gpsimd",
                   lambda e, _o=out, _i=in_: e.dma_start(out=_o, in_=_i,
                                                         accum_op=ALU.add),
                   [(sem, 16)])

        def hbcast(out, in_row, parts, sem):
            S.emit("sync",
                   lambda e, _o=out, _i=in_row, _p=parts:
                   e.dma_start(out=_o, in_=_i.partition_broadcast(_p)), [(sem, 16)])

        def bcast(out, in_row, parts, sem):
            S.emit("gpsimd",
                   lambda e, _o=out, _i=in_row, _p=parts:
                   e.dma_start(out=_o, in_=_i.partition_broadcast(_p)), [(sem, 16)])

        def mm(out, lhsT, rhs, start, stop, inc=False):
            S.emit("tensor",
                   lambda e, _o=out, _l=lhsT, _r=rhs, _s=start, _p=stop:
                   e.matmul(_o, _l, _r, start=_s, stop=_p),
                   [("T", 1)] if inc else ())

        def trans(out, in_, ident, inc=False):
            S.emit("tensor",
                   lambda e, _o=out, _i=in_, _d=ident: e.transpose(_o, _i, _d),
                   [("T", 1)] if inc else ())

        def act(out, in_, func, bias=0.0, scale=1.0, accum=None, inc=True):
            S.emit("scalar",
                   lambda e, _o=out, _i=in_, _f=func, _b=bias, _s=scale, _a=accum:
                   e.activation(_o, _i, _f, bias=_b, scale=_s, accum_out=_a),
                   [("S", 1)] if inc else ())

        def vcopy(out, in_, inc=True):
            S.emit("vector", lambda e, _o=out, _i=in_: e.tensor_copy(out=_o, in_=_i),
                   [("V", 1)] if inc else ())

        def vtt(op, out, a, b, inc=True):
            S.emit("vector",
                   lambda e, _o=out, _a=a, _b=b, _p=op:
                   e.tensor_tensor(out=_o, in0=_a, in1=_b, op=_p),
                   [("V", 1)] if inc else ())

        def vts(out, in0, s1, op0, s2=None, op1=None, inc=True):
            if op1 is None:
                S.emit("vector",
                       lambda e, _o=out, _i=in0, _s1=s1, _o0=op0:
                       e.tensor_scalar(out=_o, in0=_i, scalar1=_s1, scalar2=None,
                                       op0=_o0),
                       [("V", 1)] if inc else ())
            else:
                S.emit("vector",
                       lambda e, _o=out, _i=in0, _s1=s1, _s2=s2, _o0=op0, _o1=op1:
                       e.tensor_scalar(out=_o, in0=_i, scalar1=_s1, scalar2=_s2,
                                       op0=_o0, op1=_o1),
                       [("V", 1)] if inc else ())

        def vstt(out, in0, scalar, in1, op0, op1, inc=True):
            S.emit("vector",
                   lambda e, _o=out, _i0=in0, _sc=scalar, _i1=in1, _o0=op0, _o1=op1:
                   e.scalar_tensor_tensor(out=_o, in0=_i0, scalar=_sc, in1=_i1,
                                          op0=_o0, op1=_o1),
                   [("V", 1)] if inc else ())

        def vsw():
            S.wait("vector", "V")

        def ssw():
            S.wait("scalar", "S")

        def vmemset(ap, val, inc=True):
            S.emit("vector", lambda e, _a=ap, _v=val: e.memset(_a, _v),
                   [("V", 1)] if inc else ())

        def vred(out, in_, inc=True):
            S.emit("vector",
                   lambda e, _o=out, _i=in_: e.reduce_sum(out=_o, in_=_i, axis=AX.X),
                   [("V", 1)] if inc else ())

        try:
            abf.push()
            w_bf = {m: [abf.alloc(512, name=f"w{m}_{k}") for k in range(16)]
                    for m in range(4)}
            wab_bf = [abf.alloc(8, name=f"wab{k}") for k in range(16)]
            conv_diag = {m: [abf.alloc(512, name=f"cd{m}{oc}") for oc in range(4)]
                         for m in range(3)}
            convin = {m: [abf.alloc(516, name=f"ci{m}{oc}") for oc in range(4)]
                      for m in range(3)}
            g_sig = [abf.alloc(512, name=f"gs{oc}") for oc in range(4)]
            xn_off = abf.off
            xn_fm = abf.alloc(16 * 512, name="xn_fm")
            mask_su4 = abf.view(xn_off, 512)
            mask_iu4 = abf.view(xn_off + 512, 512)
            identb4w = abf.view(xn_off + 1024, 512)
            tmpAw = abf.view(xn_off + 1536, 512, F32)
            tmpBw = abf.view(xn_off + 2560, 512, F32)
            woj_bf = [abf.alloc(2048, name=f"woj{k}") for k in range(4)]
            og_tile = [abf.alloc(512, name=f"ogt{h}") for h in range(4)]
            rs_stage = [abf.alloc(512, name=f"rss{i}") for i in range(2)]
            xn_tok_bf = abf.alloc(2048, name="xn_tok_bf")
            x_tok = abf.alloc(2048, F32, name="x_tok")

            for src, dst in ((identb, c_identb), (identf, c_identf), (tuones, c_tuones),
                             (lnwb, c_lnw),
                             (lnbb, c_lnb), (abb, abbv), (convw, cw_f), (convb, cb_f),
                             (bupt, bupv), (bdnt, bdnv)):
                dma("sync", dst, src, "Lw")

            for k in range(16):
                dma("gpsimd", wab_bf[k], wab[128 * k:128 * k + 128, :], "Cw")
            for m, wsrc in enumerate((wq, wk, wv, wg)):
                for k in range(16):
                    dma("gpsimd", w_bf[m][k], wsrc[128 * k:128 * k + 128, :], "Cw")
            for k in range(4):
                dma("gpsimd", woj_bf[k], woj[128 * k:128 * k + 128, :], "Cy")
            for k in range(64):
                dma("gpsimd", dlab[:, 16 * k:16 * k + 16],
                    dla[128 * k:128 * k + 128, :], "Cy")
            for k in range(16):
                dma("gpsimd", ulab[:, 16 * k:16 * k + 16],
                    ula[128 * k:128 * k + 128, :], "Cy")
            dma("gpsimd", dlb_bf, dlb, "Cy")
            dma("gpsimd", supt_bf, supt, "Cy")
            dma("gpsimd", sdnt_bf, sdnt, "Cy")

            S.wait("vector", "Lw")
            for fi, fv in enumerate((1e-5, -7.5, -0.5 * LN128, -0.5)):
                S.emit("vector", lambda e, _a=fconst[:, fi:fi + 1], _v=fv: e.memset(_a, _v),
                       [("V", 1)])
            nc.const_aps.register(F32, -7.5, fconst[:, 1:2])
            for m in range(3):
                for oc in range(4):
                    for j in range(KS):
                        vts(conv_diag[m][oc][:, 128 * j:128 * j + 128], c_identf,
                            cw_f[:, (m * 4 + oc) * KS + j:(m * 4 + oc) * KS + j + 1],
                            ALU.mult, inc=False)
                    vmemset(convin[m][oc][:, 0:3], 0.0, inc=False)
                    vmemset(convin[m][oc][:, 515:516], 0.0, inc=False)
            vmemset(la_tile, 0.0)

            s_f32 = [[abf.alloc(128, F32, name=f"S{h}_{p}") for p in range(2)]
                     for h in range(H4)]
            s_bf = [abf.alloc(128, name=f"Sb{h}") for h in range(H4)]
            def wide4(name):
                base = abf.alloc(512, name=name)
                return base, [base[:, 128 * h:128 * h + 128] for h in range(H4)]

            a4w, a4 = wide4("a4")
            abarw, abar4 = wide4("ab4")
            xtw, xt4 = wide4("xt4")
            vtmw, vtm4 = wide4("vt4")
            kgw, kg4 = wide4("kg4")
            q0w, q04 = wide4("q04")
            pjw, pj4 = wide4("pj4")
            qjw, qj4 = wide4("qj4")
            qlevW = [abf.alloc(512, name=f"ql4_{i}") for i in range(4)]
            ybw = [abf.alloc(512, name=f"yb4_{i}") for i in range(2)]
            rbfw, rbf4 = a4w, a4
            n34 = q04
            w4w, w4 = wide4("w4")
            ebpQ = abf.alloc(512, F32, name="ebpQ")
            eapQ = abf.alloc(512, F32, name="eapQ")
            orw4 = [abf.alloc(128, F32, name=f"orw4_{h}") for h in range(H4)]
            rf2 = [abf.alloc(128, F32, name=f"rf2_{i}") for i in range(2)]
            otmp1 = abf.alloc(128, F32, name="otmp1")
            n1b1 = abf.alloc(128, F32, name="n1b1")
            n2b1 = abf.alloc(128, F32, name="n2b1")
            dclA = rf2
            dclB = [n1b1, n2b1]
            tmpA = dclA
            tmpB = dclB
            sjunk = otmp1

            for h in range(H4):
                vmemset(s_f32[h][0], 0.0, inc=False)
                vmemset(s_f32[h][1], 0.0, inc=False)
                vmemset(s_bf[h], 0.0, inc=(h == H4 - 1))

            cap = {}

            def ln_subtile(x_src_ap):
                S.wait("sync", "V", cap.get("pbt_free"))
                S.wait("sync", "S", cap.get("ln_s"))
                dma("sync", x_tok, x_src_ap, "Lx")
                S.wait("scalar", "Lx")
                S.wait("vector", "Lx")
                act(xn_tok_bf, x_tok, AF.Square, accum=stats[:, 0:1])
                vred(stats[:, 1:2], x_tok)
                S.wait("vector", "S")
                vsw()
                vts(stats[:, 2:3], stats[:, 1:2], 1.0 / 2048, ALU.mult, inc=False)
                vts(stats[:, 3:4], stats[:, 0:1], 1.0 / 2048, ALU.mult)
                vsw()
                vts(stats[:, 4:5], stats[:, 2:3], stats[:, 2:3], ALU.mult)
                vsw()
                vtt(ALU.subtract, stats[:, 5:6], stats[:, 3:4], stats[:, 4:5])
                S.wait("scalar", "V")
                act(stats[:, 6:7], stats[:, 5:6], AF.Ln, bias=fconst[:, 0:1])
                ssw()
                act(stats[:, 7:8], stats[:, 6:7], AF.Exp, scale=fconst[:, 3:4])
                S.wait("vector", "S")
                vts(xn_tok_bf, x_tok, stats[:, 2:3], ALU.subtract, stats[:, 7:8], ALU.mult)
                cap["ln_v"] = S.cnt["V"]
                cap["ln_s"] = S.cnt["S"]

            def xn_transpose(dst_3d):
                for half in range(2):
                    S.wait("tensor", "V", cap.get("pbt_free"))
                    S.wait("tensor", "V", cap.get("ln_v"))
                    for kk in range(8):
                        k = half * 8 + kk
                        trans(pbt[:, 128 * kk:128 * kk + 128],
                              xn_tok_bf[:, 128 * k:128 * k + 128], c_identb, inc=(kk == 7))
                    capt = S.cnt["T"]
                    S.wait("vector", "T", capt)
                    src = pbt.rearrange("p (c f) -> p c f", f=128)
                    vcopy(dst_3d[:, 8 * half:8 * half + 8, :], src)
                    cap["pbt_free"] = S.cnt["V"]

            def planes_for_tile(ti):
                col0 = 16 * ti
                sl = slice(col0, col0 + 16)
                S.wait("tensor", "V", cap.get("pbt_free"))
                S.wait("tensor", "S", cap.get("planes_pft_s"))
                for k in range(16):
                    mm(ps[2][0:8, :], wab_bf[k], xn_fm[:, 512 * k:512 * k + 512],
                       start=(k == 0), stop=(k == 15), inc=(k == 15))
                S.wait("scalar", "T")
                act(ab_sig, ps[2][0:8, :], AF.Sigmoid, bias=abbv)
                ssw()
                act(ab_ln, ab_sig, AF.Ln)
                S.wait("tensor", "S")
                for c2 in range(4):
                    trans(pft[0:128, 16 * c2:16 * c2 + 8],
                          ab_ln[:, 128 * c2:128 * c2 + 128], c_identf[0:8, 0:8], inc=False)
                    trans(pft[0:128, 16 * c2 + 8:16 * c2 + 16],
                          ab_sig[:, 128 * c2:128 * c2 + 128], c_identf[0:8, 0:8],
                          inc=(c2 == 3))
                S.wait("vector", "T")
                for c2 in range(4):
                    base = 16 * c2
                    vcopy(la_tile[:, 4 * c2:4 * c2 + 4], pft[0:128, base:base + 4],
                          inc=False)
                    vcopy(lb_tile[:, 4 * c2:4 * c2 + 4], pft[0:128, base + 4:base + 8],
                          inc=False)
                    vcopy(pl_btm[:, col0 + 4 * c2:col0 + 4 * c2 + 4],
                          pft[0:128, base + 12:base + 16], inc=(c2 == 3))
                S.wait("tensor", "V")
                mm(ps[2][:, 0:16], c_tuones, la_tile, start=True, stop=True, inc=True)
                S.wait("scalar", "T")
                act(pl_cum[:, sl], ps[2][:, 0:16], AF.Copy)
                if ti == 0 and "t_cum0" in tap_aps:
                    S.wait("sync", "S")
                    dma("sync", tap_aps["t_cum0"], pl_cum[:, 0:16], "LT")
                    dma("sync", tap_aps["t_latile"], la_tile, "LT")
                    dma("sync", tap_aps["t_tu"], c_tuones, "LT")
                    S.wait("vector", "LT")
                    S.wait("scalar", "LT")
                    S.wait("tensor", "LT")
                S.wait("vector", "S")
                vts(pl_ncumc[:, sl], pl_cum[:, sl], -1.0, ALU.mult, -LN128, ALU.add,
                    inc=False)
                vts(pl_ncumc2[:, sl], pl_cum[:, sl], -1.0, ALU.mult, -0.5 * LN128, ALU.add,
                    inc=False)
                vtt(ALU.subtract, pl_scr, pl_cum[:, sl], la_tile)
                vsw()
                vtt(ALU.add, pl_lgb[:, sl], pl_scr, lb_tile)
                S.wait("scalar", "V")
                act(pl_gam[:, sl], pl_cum[:, sl], AF.Exp, inc=False)
                act(pl_scr, pl_lgb[:, sl], AF.Exp, bias=fconst[:, 2:3])
                S.wait("vector", "S")
                vts(pl_nexplgb[:, sl], pl_scr, -1.0, ALU.mult)
                S.wait("gpsimd", "S")
                dma("gpsimd", clast_dram[0:1, sl], pl_cum[127:128, sl], "Gp")
                S.wait("gpsimd", "Gp")
                bcast(pl_clb[:, sl], clast_dram[0:1, sl], 128, "Gp")
                S.wait("vector", "Gp")
                vsw()
                vtt(ALU.subtract, pl_scr, pl_clb[:, sl], pl_cum[:, sl])
                S.wait("scalar", "V")
                act(pl_grev[:, sl], pl_scr, AF.Exp, bias=fconst[:, 2:3], inc=False)
                act(pl_gamc[:, sl], pl_clb[:, sl], AF.Exp)
                for plane, dest in ((pl_lgb, lgbt_dram), (pl_cum, cumt_dram)):
                    S.wait("tensor", "S")
                    S.wait("tensor", "V")
                    trans(pft[0:16, 0:128], plane[:, sl], c_identf, inc=True)
                    S.wait("vector", "T")
                    S.wait("vector", "Lpl")
                    vcopy(plstage, pft[0:16, 0:128])
                    S.wait("sync", "V")
                    dma("sync", dest[col0:col0 + 16, :], plstage, "Lpl")
                cap["lpl_rows"] = S.cnt["Lpl"]
                cap["planes_pft_v"] = S.cnt["V"]
                cap["planes_pft_s"] = S.cnt["S"]

            def proj_and_conv(ti):
                for m in range(4):
                    for oc in range(4):
                        g = m * 4 + oc
                        bank = ps[g % 2]
                        S.wait("tensor", "S", cap.get(f"pj{g % 2}"))
                        if g == 0:
                            S.wait("tensor", "S")
                        for k in range(16):
                            mm(bank, w_bf[m][k][:, 128 * oc:128 * oc + 128],
                               xn_fm[:, 512 * k:512 * k + 512],
                               start=(k == 0), stop=(k == 15), inc=(k == 15))
                        cap[f"pjm{g % 2}"] = S.cnt["T"]
                        S.wait("scalar", "T", cap[f"pjm{g % 2}"])
                        if m < 3:
                            act(convin[m][oc][:, 3:515], bank, AF.Copy)
                        else:
                            act(g_sig[oc], bank, AF.Sigmoid)
                        cap[f"pj{g % 2}"] = S.cnt["S"]
                        cap[f"pjsrc{m}{oc}"] = S.cnt["S"]
                for m in range(3):
                    for oc in range(4):
                        c = m * 4 + oc
                        bank = ps[2 + (c % 4)]
                        S.wait("tensor", "S", cap[f"pjsrc{m}{oc}"])
                        S.wait("tensor", "S", cap.get(f"cv{c % 4}"))
                        for j in range(KS):
                            mm(bank, conv_diag[m][oc][:, 128 * j:128 * j + 128],
                               convin[m][oc][:, j:j + 512],
                               start=(j == 0), stop=(j == KS - 1), inc=(j == KS - 1))
                        cap[f"cvm{c % 4}"] = S.cnt["T"]
                        S.wait("vector", "T", cap[f"cvm{c % 4}"])
                        vcopy(convin[m][oc][:, 0:3], convin[m][oc][:, 512:515])
                        cap[f"cvt{c % 4}"] = S.cnt["V"]
                        S.wait("scalar", "V", cap[f"cvt{c % 4}"])
                        S.wait("scalar", "T", cap[f"cvm{c % 4}"])
                        act(convin[m][oc][:, 3:515], bank, AF.Silu,
                            bias=cb_f[:, m * 4 + oc:m * 4 + oc + 1])
                        cap[f"cv{c % 4}"] = S.cnt["S"]
                cap["convin_s"] = S.cnt["S"]
                S.wait("sync", "T", cap[f"pjm0"])
                S.wait("sync", "T", cap[f"pjm1"])
                S.wait("sync", "LT")
                dma("sync", mask_su4, masksu4, "Lm")
                dma("sync", mask_iu4, maskiu4, "Lm")
                dma("sync", identb4w, identb4, "Lm")
                cap["mask_ld"] = S.cnt["Lm"]

            def ebcast(ci):
                S.wait("gpsimd", "V", cap.get("eq_free"))
                S.wait("gpsimd", "Lpl", cap.get("lpl_rows"))
                for h in range(H4):
                    c = 4 * ci + h
                    bcast(ebpQ[:, 128 * h:128 * h + 128],
                          lgbt_dram[c:c + 1, :], 128, "GpA")
                    bcast(eapQ[:, 128 * h:128 * h + 128],
                          cumt_dram[c:c + 1, :], 128, "GpA")
                cap[f"gpE{ci}"] = S.cnt["GpA"]

            def heads_cols(ci):
                c2 = ci % 4
                qf = [convin[0][h][:, 3 + 128 * c2:3 + 128 * c2 + 128]
                      for h in range(H4)]
                kf = [convin[1][h][:, 3 + 128 * c2:3 + 128 * c2 + 128]
                      for h in range(H4)]
                vf = [convin[2][h][:, 3 + 128 * c2:3 + 128 * c2 + 128]
                      for h in range(H4)]
                col = [4 * ci + h for h in range(H4)]
                return qf, kf, vf, col

            def t1_part(ci):
                _, kf, _, _ = heads_cols(ci)
                qf = heads_cols(ci)[0]
                S.wait("tensor", "S", cap.get("convin_s"))
                if ci % 4 == 0:
                    S.wait("tensor", "V", cap.get("planes_pft_v"))
                    S.wait("tensor", "S", cap.get("planes_pft_s"))
                for h in range(H4):
                    mm(ps[5][:, 128 * h:128 * h + 128], kf[h], kf[h], start=True,
                       stop=True)
                    mm(pft[:, 128 * h:128 * h + 128], kf[h], qf[h], start=True,
                       stop=True, inc=(h == 3))
                cap[f"t1_{ci}"] = S.cnt["T"]

            def loopA_part(ci, guard_sq6=None):
                _, _, _, col = heads_cols(ci)
                S.wait("vector", "GpA", cap[f"gpE{ci}"])
                for h in range(H4):
                    p = h % 2
                    ebs = ebpQ[:, 128 * h:128 * h + 128]
                    eas = eapQ[:, 128 * h:128 * h + 128]
                    S.wait("vector", "S", cap.get(f"exp_p{p}"))
                    vts(dclA[p], ebs, pl_ncumc[:, col[h]:col[h] + 1],
                        ALU.add, 25.0, ALU.min, inc=False)
                    vts(dclB[p], eas, pl_ncumc2[:, col[h]:col[h] + 1],
                        ALU.add, 25.0, ALU.min)
                    cap[f"clamp_p{p}"] = S.cnt["V"]
                    S.wait("scalar", "V", cap[f"clamp_p{p}"])
                    act(ebs, dclA[p], AF.Exp, inc=False)
                    act(eas, dclB[p], AF.Exp)
                    cap[f"exp_p{p}"] = S.cnt["S"]
                S.wait("vector", "T", cap[f"t1_{ci}"])
                if guard_sq6 is not None:
                    S.wait("vector", "T", guard_sq6)
                S.wait("vector", "S", cap["exp_p0"])
                S.wait("vector", "S", cap["exp_p1"])
                S.wait("vector", "Lm", cap.get("mask_ld"))
                vtt(ALU.mult, tmpAw, ps[5][:, 0:512], ebpQ, inc=False)
                vtt(ALU.mult, tmpBw, pft[:, 0:512], eapQ)
                vsw()
                vtt(ALU.mult, a4w, tmpAw, mask_su4, inc=False)
                vtt(ALU.mult, abarw, tmpBw, mask_iu4)
                cap["eq_free"] = S.cnt["V"]
                cap[f"abuild_{ci}"] = S.cnt["V"]
                if ci % 4 < 3:
                    ebcast(ci + 1)

            def rest_pre(ci):
                _, _, vf, col = heads_cols(ci)
                S.wait("tensor", "V", cap.get("pbt_free"))
                for h in range(H4):
                    trans(pbt[:, 128 * h:128 * h + 128], vf[h], c_identb, inc=False)
                    trans(pbt[:, 512 + 128 * h:512 + 128 * h + 128],
                          convin[1][h][:, 3 + 128 * (ci % 4):3 + 128 * (ci % 4) + 128],
                          c_identb, inc=(h == 3))
                cap["t2a"] = S.cnt["T"]
                S.wait("vector", "T", cap["t2a"])
                for h in range(H4):
                    vts(vtm4[h], pbt[:, 128 * h:128 * h + 128],
                        pl_btm[:, col[h]:col[h] + 1], ALU.mult, inc=False)
                    vts(kg4[h], pbt[:, 512 + 128 * h:512 + 128 * h + 128],
                        pl_grev[:, col[h]:col[h] + 1], ALU.mult, inc=(h == 3))
                cap["v3"] = S.cnt["V"]
                S.wait("tensor", "V", cap["v3"])
                for h in range(H4):
                    trans(pbt[:, 128 * h:128 * h + 128], a4[h], c_identb, inc=(h == 3))
                cap["t2b"] = S.cnt["T"]
                S.wait("vector", "T", cap["t2b"])
                vcopy(q0w, pbt[:, 0:512], inc=False)
                vcopy(pjw, a4w, inc=False)
                vcopy(qjw, pbt[:, 0:512])
                cap["v4"] = S.cnt["V"]
                cap["pbt_free"] = S.cnt["V"]
                prevv = cap["v4"]
                for lev in range(1, 7):
                    S.wait("tensor", "V", prevv)
                    if lev == 1:
                        S.wait("tensor", "V", cap.get("wo_free"))
                    for h in range(H4):
                        mm(ps[2][:, 128 * h:128 * h + 128], qj4[h], pj4[h],
                           start=True, stop=True, inc=(lev == 6 and h == 3))
                        if lev < 6:
                            mm(ps[3][:, 128 * h:128 * h + 128], pj4[h], qj4[h],
                               start=True, stop=True, inc=(h == 3))
                    capt = S.cnt["T"]
                    S.wait("vector", "T", capt)
                    if lev < 6:
                        if lev >= 2:
                            vcopy(qlevW[lev - 2], qjw)
                        vsw()
                        vcopy(pjw, ps[2][:, 0:512], inc=False)
                        vcopy(qjw, ps[3][:, 0:512])
                    else:
                        vcopy(pjw, ps[2][:, 0:512])
                    prevv = S.cnt["V"]
                vsw()
                vtt(ALU.add, ybw[0], identb4w, pjw)
                prevv = S.cnt["V"]
                for i, lev in enumerate(range(5, 0, -1)):
                    S.wait("tensor", "V", prevv)
                    for h in range(H4):
                        qsrc = (qj4[h] if lev == 5
                                else qlevW[lev - 1][:, 128 * h:128 * h + 128])
                        mm(ps[2][:, 128 * h:128 * h + 128], qsrc,
                           ybw[i % 2][:, 128 * h:128 * h + 128],
                           start=True, stop=True, inc=(h == 3))
                    capt = S.cnt["T"]
                    S.wait("vector", "T", capt)
                    vtt(ALU.add, ybw[(i + 1) % 2], ps[2][:, 0:512], ybw[i % 2])
                    prevv = S.cnt["V"]
                S.wait("tensor", "V", prevv)
                for h in range(H4):
                    mm(ps[2][:, 128 * h:128 * h + 128], q04[h],
                       ybw[1][:, 128 * h:128 * h + 128],
                       start=True, stop=True, inc=(h == 3))
                capt = S.cnt["T"]
                S.wait("vector", "T", capt)
                vtt(ALU.subtract, xtw, ybw[1], ps[2][:, 0:512])
                cap[f"xt_{ci}"] = S.cnt["V"]

            def pre_chunk(ci):
                if ci % 4 == 0:
                    ebcast(ci)
                t1_part(ci)
                loopA_part(ci)
                rest_pre(ci)

            def seq_chunk(ci, nxt):
                par = ci % 2
                qf, kf, _, col = heads_cols(ci)
                S.wait("tensor", "V", cap.get(f"sq3_{ci - 1}"))
                S.wait("tensor", "V", cap.get(f"sbf_{ci - 1}"))
                for h in range(H4):
                    mm(ps[0][:, 128 * h:128 * h + 128], kf[h], s_bf[h], start=True,
                       stop=True, inc=(h == 3))
                cap["sq1"] = S.cnt["T"]
                S.wait("vector", "T", cap["sq1"])
                for h in range(H4):
                    vstt(rbf4[h], ps[0][:, 128 * h:128 * h + 128],
                         pl_nexplgb[:, col[h]:col[h] + 1], vtm4[h],
                         ALU.mult, ALU.add)
                cap[f"sq3_{ci}"] = S.cnt["V"]
                S.wait("tensor", "V", cap[f"sq3_{ci}"])
                for h in range(H4):
                    mm(ps[1][:, 128 * h:128 * h + 128], xt4[h], rbf4[h], start=True,
                       stop=True, inc=(h == 3))
                cap["sq4"] = S.cnt["T"]
                S.wait("vector", "T", cap["sq4"])
                vcopy(w4w, ps[1][:, 0:512])
                cap[f"sq5_{ci}"] = S.cnt["V"]
                if nxt is not None:
                    t1_part(nxt)
                S.wait("tensor", "V", cap[f"sq5_{ci}"])
                for h in range(H4):
                    mm(ps[2][:, 128 * h:128 * h + 128], qf[h], s_bf[h], start=True,
                       stop=True)
                    mm(ps[3][:, 128 * h:128 * h + 128], abar4[h], w4[h], start=True,
                       stop=True)
                    mm(ps[4][:, 128 * h:128 * h + 128], kg4[h], w4[h], start=True,
                       stop=True, inc=(h == 3))
                cap[f"sq6_{ci}"] = S.cnt["T"]
                S.wait("vector", "T", cap[f"sq6_{ci}"])
                for h in range(H4):
                    vts(otmp1, ps[2][:, 128 * h:128 * h + 128],
                        pl_gam[:, col[h]:col[h] + 1], ALU.mult)
                    vsw()
                    vtt(ALU.add, orw4[h], ps[3][:, 128 * h:128 * h + 128], otmp1)
                    vstt(s_f32[h][par], s_f32[h][1 - par],
                         pl_gamc[:, col[h]:col[h] + 1],
                         ps[4][:, 128 * h:128 * h + 128], ALU.mult, ALU.add)
                    vsw()
                for h in range(H4):
                    vcopy(s_bf[h], s_f32[h][par], inc=(h == 3))
                cap[f"sbf_{ci}"] = S.cnt["V"]
                S.wait("scalar", "V", cap[f"sbf_{ci}"])
                for h in range(H4):
                    if h:
                        ssw()
                    act(sjunk, orw4[h], AF.Square, accum=stats[:, 16 + h:17 + h])
                cap["sq8"] = S.cnt["S"]
                if nxt is not None:
                    loopA_part(nxt, guard_sq6=cap[f"sq6_{ci}"])
                S.wait("vector", "S", cap["sq8"])
                for h in range(H4):
                    vred(stats[:, 20 + h:21 + h], orw4[h], inc=(h == 3))
                vsw()
                for h in range(H4):
                    vts(stats[:, 24 + h:25 + h], stats[:, 20 + h:21 + h], 1.0 / 128,
                        ALU.mult, inc=False)
                    vts(stats[:, 28 + h:29 + h], stats[:, 16 + h:17 + h], 1.0 / 128,
                        ALU.mult, inc=(h == 3))
                vsw()
                for h in range(H4):
                    vts(stats[:, 32 + h:33 + h], stats[:, 24 + h:25 + h],
                        stats[:, 24 + h:25 + h], ALU.mult, inc=(h == 3))
                vsw()
                for h in range(H4):
                    vtt(ALU.subtract, stats[:, 36 + h:37 + h],
                        stats[:, 28 + h:29 + h], stats[:, 32 + h:33 + h],
                        inc=(h == 3))
                cap["sq9"] = S.cnt["V"]
                S.wait("scalar", "V", cap["sq9"])
                for h in range(H4):
                    act(stats[:, 40 + h:41 + h], stats[:, 36 + h:37 + h], AF.Ln,
                        bias=fconst[:, 0:1], inc=(h == 3))
                ssw()
                for h in range(H4):
                    act(stats[:, 44 + h:45 + h], stats[:, 40 + h:41 + h], AF.Exp,
                        scale=fconst[:, 3:4], inc=(h == 3))
                cap["sq10"] = S.cnt["S"]
                S.wait("vector", "S", cap["sq10"])
                for h in range(H4):
                    vts(n1b1, orw4[h], stats[:, 24 + h:25 + h], ALU.subtract,
                        stats[:, 44 + h:45 + h], ALU.mult)
                    vsw()
                    vtt(ALU.mult, n2b1, n1b1, c_lnw)
                    vsw()
                    vtt(ALU.add, n34[h], n2b1, c_lnb, inc=(h == 3))
                cap["sq11"] = S.cnt["V"]
                S.wait("tensor", "V", cap["sq11"])
                for h in range(H4):
                    trans(pbt[:, 128 * h:128 * h + 128], n34[h], c_identb,
                          inc=(h == 3))
                cap["sq12"] = S.cnt["T"]
                S.wait("vector", "T", cap["sq12"])
                for h in range(H4):
                    vtt(ALU.mult, og_tile[h][:, 128 * (ci % 4):128 * (ci % 4) + 128],
                        pbt[:, 128 * h:128 * h + 128],
                        g_sig[h][:, 128 * (ci % 4):128 * (ci % 4) + 128],
                        inc=(h == 3))
                cap["og"] = S.cnt["V"]
                cap["pbt_free"] = S.cnt["V"]
                if nxt is not None:
                    rest_pre(nxt)

            S.wait("tensor", "Lw")
            S.wait("scalar", "Lw")
            S.wait("gpsimd", "Lw")
            def wo_epi(ti):
                q4 = ti // 2
                thh = ti % 2
                S.wait("tensor", "Cy")
                S.wait("tensor", "S")
                S.wait("tensor", "V", cap["og"])
                for dc in range(16):
                    bank = ps[3 + dc % 2]
                    rot = dc % 2
                    S.wait("tensor", "V", cap.get(f"wo_bk{rot}"))
                    for k in range(4):
                        mm(bank, woj_bf[k][:, 128 * dc:128 * dc + 128], og_tile[k],
                           start=(k == 0), stop=(k == 3), inc=(k == 3))
                    cap[f"wo_mm{rot}"] = S.cnt["T"]
                    S.wait("vector", "T", cap[f"wo_mm{rot}"])
                    S.wait("vector", "Lpl")
                    vcopy(rs_stage[rot], bank)
                    cap[f"wo_bk{rot}"] = S.cnt["V"]
                    S.wait("sync", "V", cap[f"wo_bk{rot}"])
                    dma("sync", rs_in[2048 * q4 + 128 * dc:2048 * q4 + 128 * dc + 128,
                                      512 * thh:512 * thh + 512], rs_stage[rot],
                        "Lpl")
                    if ti == 0 and dc == 0 and "t_rstg" in tap_aps:
                        dma("sync", tap_aps["t_rstg"], rs_stage[rot], "LT")
                cap["wo_free"] = S.cnt["V"]

            xn_fm_3d = xn_fm.rearrange("p (c f) -> p c f", f=512)
            for ti in range(NT):
                for s4 in range(4):
                    t0 = ti * TILE + s4 * 128
                    ln_subtile(xtm[t0:t0 + 128, :])
                    xn_transpose(xn_fm_3d[:, :, 128 * s4:128 * s4 + 128])
                if ti == 0 and "t_xn" in tap_aps:
                    S.wait("sync", "V")
                    dma("sync", tap_aps["t_xn"].rearrange("(c p) t -> p c t", p=128),
                        xn_fm_3d, "LT")
                if ti == 0:
                    ckpt("ln")
                if ti == 0:
                    S.wait("tensor", "Cw")
                    S.wait("scalar", "Cw")
                    S.wait("gpsimd", "Cw")
                planes_for_tile(ti)
                if ti == 0:
                    ckpt("planes")
                proj_and_conv(ti)
                if ti == 0:
                    ckpt("proj")
                if ti == 0:
                    for nm, tiles in (("t_q", convin[0]), ("t_k", convin[1]),
                                      ("t_v", convin[2]), ("t_g", g_sig)):
                        if nm in tap_aps:
                            S.wait("sync", "S")
                            w = tap_aps[nm].shape[1]
                            for oc in range(4):
                                dma("sync", tap_aps[nm][128 * oc:128 * oc + 128, :],
                                    tiles[oc][:, 0:w], "LT")
                    if "t_ab" in tap_aps:
                        S.wait("sync", "S")
                        dma("sync", tap_aps["t_ab"][0:8, :], ab_sig, "LT")
                        dma("sync", tap_aps["t_ab"][8:16, :], ab_ln, "LT")
                    for nm in ("t_q", "t_k", "t_v", "t_g", "t_ab"):
                        if nm in tap_aps:
                            S.wait("scalar", "LT")
                            S.wait("vector", "LT")
                            break
                if ti > 0:
                    wo_epi(ti - 1)
                pre_chunk(ti * 4)
                for c2 in range(4):
                    ci = ti * 4 + c2
                    seq_chunk(ci, ci + 1 if c2 < 3 else None)
                    if ci == 0:
                        ckpt("seq0")
                if ti == NT - 1:
                    wo_epi(ti)
                if ti == 0 and "t_og" in tap_aps:
                    S.wait("sync", "V")
                    for hh in range(4):
                        dma("sync", tap_aps["t_og"][128 * hh:128 * hh + 128, :],
                            og_tile[hh], "LT")
                    S.wait("vector", "LT")
                    S.wait("tensor", "LT")

            for nm, pl in (("t_cum", pl_cum), ("t_btm", pl_btm), ("t_grev", pl_grev),
                           ("t_gam", pl_gam), ("t_gamc", pl_gamc),
                           ("t_nexplgb", pl_nexplgb)):
                if nm in tap_aps:
                    S.wait("sync", "V")
                    S.wait("sync", "S")
                    S.wait("sync", "Gp")
                    dma("sync", tap_aps[nm], pl, "LT")
            if "t_lgbt" in tap_aps:
                S.wait("sync", "Lpl")
                dma("sync", tap_aps["t_lgbt"], lgbt_dram, "LT")
                dma("sync", tap_aps["t_cumt"], cumt_dram, "LT")
            if "t_rsin" in tap_aps:
                S.wait("sync", "Lpl")
                dma("sync", tap_aps["t_rsin"], rs_in, "LT")
                S.wait("gpsimd", "LT")
            ckpt("A")
            S.wait("gpsimd", "Lpl")
            S.emit("gpsimd", lambda e: e.collective_compute(
                "ReduceScatter", ALU.add,
                replica_groups=[[0, 1, 2, 3], [4, 5, 6, 7]],
                ins=[rs_in.opt()], outs=[rs_out.opt()]), [("CC", 1)])

            S.barrier()
            abf.pop()
            abf.push()
            xnb = [abf.alloc(1024, name=f"xnb{k}") for k in range(16)]
            abf.push()
            x_tok = abf.alloc(2048, F32, name="x_tokB")
            sq_scr = abf.alloc(2048, F32, name="sq_scrB")
            xn_tok_bf = abf.alloc(2048, name="xn_tokB")

            for s8 in range(8):
                ln_subtile(xtmj[128 * s8:128 * s8 + 128, :])
                for half in range(2):
                    S.wait("tensor", "V")
                    for kk in range(8):
                        k = half * 8 + kk
                        trans(pbt[:, 128 * kk:128 * kk + 128],
                              xn_tok_bf[:, 128 * k:128 * k + 128], c_identb, inc=(kk == 7))
                    S.wait("vector", "T")
                    for kk in range(8):
                        k = half * 8 + kk
                        vcopy(xnb[k][:, 128 * s8:128 * s8 + 128],
                              pbt[:, 128 * kk:128 * kk + 128], inc=(kk == 7))
                    cap["pbt_free"] = S.cnt["V"]
            S.barrier(engines=("sync", "vector", "tensor"), sems=("V", "T", "Lx"))
            abf.pop()
            wupb = [abf.alloc(1024, name=f"wupb{k}") for k in range(16)]
            wdnb = [abf.alloc(2048, name=f"wdnb{k}") for k in range(8)]
            x2 = [abf.alloc(1024, name=f"x2_{k}") for k in range(8)]
            w8u_rot = [abf.alloc(1024, I8, name=f"w8u{i}") for i in range(4)]
            w8d_rot = [abf.alloc(2048, I8, name=f"w8d{i}") for i in range(4)]
            scld_rot = [abf.alloc(2048, name=f"scld{i}") for i in range(4)]
            supt_sb = abf.alloc(HF, name="supt_sb")[0:32, :]
            selu_sb = abf.alloc(2048, name="selu_sb")[0:32, :]
            wtmp = abf.alloc(2048, name="wtmpB")
            y2st = [abf.alloc(512, F32, name=f"y2st{i}") for i in range(2)]

            ckpt("bpro")
            dma("sync", supt_sb, supt_bf, "Lw")
            dma("sync", selu_sb, sel_up, "Lw")
            cap["usc_ld"] = S.cnt["Lw"]
            S.wait("tensor", "V")
            for k in range(16):
                for th in range(2):
                    mm(ps[th][0:16, :], ulab[:, 16 * k:16 * k + 16],
                       xnb[k][:, 512 * th:512 * th + 512],
                       start=(k == 0), stop=(k == 15), inc=(k == 15 and th == 1))
            S.wait("vector", "T")
            vcopy(l1_bf[:, 0:512], ps[0][0:16, :], inc=False)
            vcopy(l1_bf[:, 512:1024], ps[1][0:16, :], inc=False)
            vmemset(l2_f, 0.0)

            ckpt("l1")
            for hg in range(8):
                if hg == 1:
                    ckpt("hg0")
                S.wait("gpsimd", "T", cap.get("ulbb_free"))
                dma("gpsimd", ulbb, ulb[:, 1024 * hg:1024 * hg + 1024], "Cw")
                cap["ulbb_ld"] = S.cnt["Cw"]
                S.wait("vector", "T", cap.get("upmm_last"))
                for k in range(16):
                    rot = k % 4
                    q = ("QuA", "QuB", "QuC", "QuD")[rot]
                    S.wait("sync", "V", cap.get(f"uq_free{rot}"))
                    dma("sync", w8u_rot[rot],
                        wup8[128 * k:128 * k + 128, 1024 * hg:1024 * hg + 1024], q)
                    cap[f"uq{rot}"] = S.cnt[q]
                    bkA, bkB = (ps[4], ps[5]) if k % 2 == 0 else (ps[2], ps[3])
                    S.wait("tensor", "Lw", cap["usc_ld"])
                    S.wait("tensor", "V", cap.get(f"usc_free{k % 2}"))
                    if k < 2:
                        for b in range(4):
                            S.wait("tensor", "V", cap.get(f"dnevac{b}"))
                    for half in range(2):
                        bk = bkA if half == 0 else bkB
                        mm(bk[:, 0:512], selu_sb[:, 128 * k:128 * k + 128],
                           supt_sb[:, 1024 * hg + 512 * half:
                                   1024 * hg + 512 * half + 512],
                           start=True, stop=True, inc=(half == 1))
                    cap[f"uscmm{k % 2}"] = S.cnt["T"]
                    S.wait("vector", q, cap[f"uq{rot}"])
                    S.wait("vector", "T", cap[f"uscmm{k % 2}"])
                    vstt(wupb[k][:, 0:512], w8u_rot[rot][:, 0:512], -7.5,
                         bkA[:, 0:512], ALU.add, ALU.mult, inc=False)
                    vstt(wupb[k][:, 512:1024], w8u_rot[rot][:, 512:1024], -7.5,
                         bkB[:, 0:512], ALU.add, ALU.mult)
                    cap[f"uq_free{rot}"] = S.cnt["V"]
                    cap[f"usc_free{k % 2}"] = S.cnt["V"]
                    cap[f"updqk{k}"] = S.cnt["V"]
                cap["updq_ps"] = S.cnt["V"]
                S.wait("tensor", "Cw", cap["ulbb_ld"])
                g = 0
                for hc in range(8):
                    for th in range(2):
                        bank = ps[g % 2]
                        S.wait("tensor", "S", cap.get(f"upevac{g % 2}"))
                        for k in range(16):
                            if g == 0:
                                S.wait("tensor", "V", cap[f"updqk{k}"])
                            mm(bank, wupb[k][:, 128 * hc:128 * hc + 128],
                               xnb[k][:, 512 * th:512 * th + 512],
                               start=(k == 0), stop=False, inc=False)
                        mm(bank, ulbb[:, 128 * hc:128 * hc + 128],
                           l1_bf[:, 512 * th:512 * th + 512],
                           start=False, stop=True, inc=True)
                        cap[f"upmm{g % 2}"] = S.cnt["T"]
                        S.wait("scalar", "T", cap[f"upmm{g % 2}"])
                        if g == 0:
                            S.wait("scalar", "T", cap.get("dnmm_last"))
                        act(x2[hc][:, 512 * th:512 * th + 512], bank, AF.Relu,
                            bias=bupv[:, hg * 8 + hc:hg * 8 + hc + 1])
                        cap[f"upevac{g % 2}"] = S.cnt["S"]
                        g += 1
                cap["upmm_last"] = S.cnt["T"]
                cap["ulbb_free"] = S.cnt["T"]
                S.wait("tensor", "S", cap["upevac0"])
                S.wait("tensor", "S", cap["upevac1"])
                S.wait("tensor", "V", cap.get("dnevac0"))
                S.wait("tensor", "V", cap.get("dnevac1"))
                S.wait("tensor", "V", cap["updq_ps"])
                for th in range(2):
                    for hc in range(8):
                        mm(ps[2][0:16, :] if th == 0 else ps[3][0:16, :],
                           dlab[:, 16 * (8 * hg + hc):16 * (8 * hg + hc) + 16],
                           x2[hc][:, 512 * th:512 * th + 512],
                           start=(hc == 0), stop=(hc == 7), inc=(hc == 7))
                cap["l2mm"] = S.cnt["T"]
                S.wait("vector", "T", cap["l2mm"])
                vtt(ALU.add, l2_tmp[:, 0:512], l2_f[:, 0:512], ps[2][0:16, :],
                    inc=False)
                vtt(ALU.add, l2_tmp[:, 512:1024], l2_f[:, 512:1024], ps[3][0:16, :])
                vsw()
                vcopy(l2_f, l2_tmp)
                cap["dnevac0"] = S.cnt["V"]
                cap["dnevac1"] = S.cnt["V"]
                S.wait("vector", "T", cap.get("dnmm_last"))
                for hc in range(8):
                    rot = hc % 4
                    q = ("QdA", "QdB", "QdC", "QdD")[rot]
                    hrow = 1024 * hg + 128 * hc
                    S.wait("sync", "V", cap.get(f"dq_free{rot}"))
                    dma("sync", w8d_rot[rot], wdn8[hrow:hrow + 128, :], q)
                    for pp in range(2):
                        hbcast(scld_rot[rot][64 * pp:64 * pp + 64, :],
                               sdnt_bf[2 * (8 * hg + hc) + pp:
                                       2 * (8 * hg + hc) + pp + 1, :], 64, q)
                    cap[f"dq{rot}"] = S.cnt[q]
                    S.wait("vector", q, cap[f"dq{rot}"])
                    vstt(wdnb[hc], w8d_rot[rot], -7.5, scld_rot[rot],
                         ALU.add, ALU.mult)
                    cap[f"dq_free{rot}"] = S.cnt["V"]
                    cap[f"dndqk{hc}"] = S.cnt["V"]
                g = 0
                for dc in range(16):
                    for th in range(2):
                        bank = ps[2 + g % 4]
                        S.wait("tensor", "V", cap.get(f"dnevac{g % 4}"))
                        for hc in range(8):
                            if g == 0:
                                S.wait("tensor", "V", cap[f"dndqk{hc}"])
                            mm(bank, wdnb[hc][:, 128 * dc:128 * dc + 128],
                               x2[hc][:, 512 * th:512 * th + 512],
                               start=(hc == 0), stop=(hc == 7), inc=(hc == 7))
                        cap[f"dnmm{g % 4}"] = S.cnt["T"]
                        rot = g % 2
                        gy = "GyA" if rot == 0 else "GyB"
                        S.wait("vector", "T", cap[f"dnmm{g % 4}"])
                        S.wait("vector", gy, cap.get(f"y2st{rot}"))
                        vcopy(y2st[rot], bank)
                        cap[f"dnevac{g % 4}"] = S.cnt["V"]
                        S.wait("gpsimd", "V", cap[f"dnevac{g % 4}"])
                        if hg == 0:
                            dma("gpsimd",
                                y2_dram[128 * dc:128 * dc + 128,
                                        512 * th:512 * th + 512],
                                y2st[rot], gy)
                        else:
                            dma_acc(
                                y2_dram[128 * dc:128 * dc + 128,
                                        512 * th:512 * th + 512],
                                y2st[rot], gy)
                        cap[f"y2st{rot}"] = S.cnt[gy]
                        g += 1
                cap["dnmm_last"] = S.cnt["T"]

            ckpt("bdone")
            S.wait("vector", "T")
            vsw()
            vcopy(l2_bf, l2_f)
            if "t_y2" in tap_aps:
                S.wait("sync", "GyA")
                S.wait("sync", "GyB")
                dma("sync", tap_aps["t_y2"], y2_dram, "LT")
            if "t_l1" in tap_aps:
                S.wait("sync", "V")
                dma("sync", tap_aps["t_l1"], l1_bf, "LT")
            if "t_l2f" in tap_aps:
                S.wait("sync", "V")
                dma("sync", tap_aps["t_l2f"], l2_f, "LT")
            if "t_x2" in tap_aps:
                S.wait("sync", "S")
                for hc in range(8):
                    dma("sync", tap_aps["t_x2"][128 * hc:128 * hc + 128, :], x2[hc], "LT")
            S.barrier()
            abf.pop()
            xr_f = [abf.alloc(512, F32, name=f"xr_f{i}") for i in range(2)]
            outf = [abf.alloc(512, F32, name=f"outf{i}") for i in range(2)]
            t1_f = [abf.alloc(512, F32, name=f"t1_f{i}") for i in range(2)]
            t2_f = [abf.alloc(512, F32, name=f"t2_f{i}") for i in range(2)]
            y2r = [abf.alloc(512, F32, name=f"y2r{i}") for i in range(2)]
            at_bf = [abf.alloc(512, name=f"atb{i}") for i in range(2)]

            ckpt("wo")
            S.wait("sync", "CC", 1)
            if "t_ofull" in tap_aps:
                dma("sync", tap_aps["t_ofull"], rs_out, "LT")

            ckpt("og")
            S.wait("sync", "GyA")
            S.wait("sync", "GyB")
            cap["cv0"] = None
            for dc in range(16):
                for th in range(2):
                    g = dc * 2 + th
                    rot = g % 2
                    bank = ps[rot]
                    lc = "LcA" if rot == 0 else "LcB"
                    st = "STA" if rot == 0 else "STB"
                    S.wait("sync", "V", cap.get(f"c_used{rot}"))
                    dma("sync", xr_f[rot], xfmj[128 * dc:128 * dc + 128,
                                               512 * th:512 * th + 512], lc)
                    dma("sync", y2r[rot],
                        y2_dram[128 * dc:128 * dc + 128, 512 * th:512 * th + 512],
                        lc)
                    dma("sync", at_bf[rot],
                        rs_out[128 * dc:128 * dc + 128, 512 * th:512 * th + 512],
                        lc)
                    cap[f"c_ld{rot}"] = S.cnt[lc]
                    S.wait("tensor", "V", cap.get(f"c_bank{rot}"))
                    if g == 0:
                        S.wait("tensor", "V")
                    mm(bank, dlb_bf[:, 128 * dc:128 * dc + 128],
                       l2_bf[:, 512 * th:512 * th + 512], start=True, stop=True,
                       inc=True)
                    cap[f"c_mm{rot}"] = S.cnt["T"]
                    S.wait("vector", "T", cap[f"c_mm{rot}"])
                    S.wait("vector", lc, cap[f"c_ld{rot}"])
                    vtt(ALU.add, t1_f[rot], bank, y2r[rot])
                    cap[f"c_bank{rot}"] = S.cnt["V"]
                    vsw()
                    vtt(ALU.add, t2_f[rot], t1_f[rot], at_bf[rot])
                    vsw()
                    S.wait("vector", st, cap.get(f"c_st{rot}"))
                    vstt(outf[rot], xr_f[rot], bdnv[:, dc:dc + 1], t2_f[rot],
                         ALU.add, ALU.add)
                    cap[f"c_out{rot}"] = S.cnt["V"]
                    cap[f"c_used{rot}"] = S.cnt["V"]
                    S.wait("sync", "V", cap[f"c_out{rot}"])
                    dma("sync", out_d[128 * dc:128 * dc + 128,
                                      512 * th:512 * th + 512], outf[rot], st)
                    cap[f"c_st{rot}"] = S.cnt[st]
            for _s in ("Lw", "Cw", "Lx", "Lm", "Lpl", "Lu8", "Ld8", "Lc", "LT", "ST",
                       "QuA", "QuB", "QuC", "QuD", "QdA", "QdB", "QdC", "QdD",
                       "LcA", "LcB", "STA", "STB"):
                S.wait("sync", _s)
            for _s in ("Gs", "Gp", "GpA", "GpB", "Gy", "GyA", "GyB", "CC", "ST",
                       "Cy"):
                S.wait("gpsimd", _s)

            ckpt("full")
        except _StopSched:
            pass
        with nc.Block() as block:
            block.sync(lambda e: [t(e) for t in S.q["sync"]])
            block.scalar(lambda e: [t(e) for t in S.q["scalar"]])
            block.vector(lambda e: [t(e) for t in S.q["vector"]])
            block.tensor(lambda e: [t(e) for t in S.q["tensor"]])
            block.gpsimd(lambda e: [t(e) for t in S.q["gpsimd"]])

    return nc



def _mk_sel_up():
    sel = np.zeros((32, 2048), np.float32)
    for k in range(16):
        sel[2 * k, 128 * k:128 * k + 64] = 1.0
        sel[2 * k + 1, 128 * k + 64:128 * k + 128] = 1.0
    import ml_dtypes
    return np.ascontiguousarray(sel).astype(ml_dtypes.bfloat16)


def _prep_core_inputs(inputs):
    bf16 = ml_dtypes.bfloat16
    x = np.asarray(inputs["L0_attn_x"], dtype=np.float32)
    Wq = np.asarray(inputs["L0_attn_Wq"], np.float32)
    Wk = np.asarray(inputs["L0_attn_Wk"], np.float32)
    Wv = np.asarray(inputs["L0_attn_Wv"], np.float32)
    Wg = np.asarray(inputs["L0_attn_Wg"], np.float32)
    Wa = np.asarray(inputs["L0_attn_Wa"], np.float32)
    Wb = np.asarray(inputs["L0_attn_Wb"], np.float32)
    ba = np.asarray(inputs["L0_attn_ba"], np.float32)
    bb = np.asarray(inputs["L0_attn_bb"], np.float32)
    Wo = np.ascontiguousarray(np.asarray(inputs["L0_attn_Wo"], np.float32))
    cqw = np.asarray(inputs["L0_attn_conv_q_w"], np.float32)[:, 0, :]
    ckw = np.asarray(inputs["L0_attn_conv_k_w"], np.float32)[:, 0, :]
    cvw = np.asarray(inputs["L0_attn_conv_v_w"], np.float32)[:, 0, :]
    cqb = np.asarray(inputs["L0_attn_conv_q_b"], np.float32)
    ckb = np.asarray(inputs["L0_attn_conv_k_b"], np.float32)
    cvb = np.asarray(inputs["L0_attn_conv_v_b"], np.float32)
    lnw = np.asarray(inputs["L0_attn_ln_w"], np.float32)
    lnb = np.asarray(inputs["L0_attn_ln_b"], np.float32)
    upq = np.asarray(inputs["L0_ffn_w_up_q"], np.int32)
    ups = np.asarray(inputs["L0_ffn_w_up_scale"], np.float32)
    bup = np.asarray(inputs["L0_ffn_b_up"], np.float32)
    ula_ = np.asarray(inputs["L0_ffn_w_up_lora_a"], np.float32)
    ulb_ = np.asarray(inputs["L0_ffn_w_up_lora_b"], np.float32)
    dnq = np.asarray(inputs["L0_ffn_w_down_q"], np.int32)
    dns = np.asarray(inputs["L0_ffn_w_down_scale"], np.float32)
    bdn = np.asarray(inputs["L0_ffn_b_down"], np.float32)
    dla_ = np.asarray(inputs["L0_ffn_w_down_lora_a"], np.float32)
    dlb_ = np.asarray(inputs["L0_ffn_w_down_lora_b"], np.float32)

    upq_t = np.ascontiguousarray(upq.T)
    wup8 = np.ascontiguousarray(upq_t.view(np.int8)[:, ::4])
    dnq_t = np.ascontiguousarray(dnq.T)
    wdn8 = np.ascontiguousarray(dnq_t.view(np.int8)[:, ::4])

    shared = dict(
        lnwb=np.ascontiguousarray(np.broadcast_to(lnw[None, :], (128, 128))),
        lnbb=np.ascontiguousarray(np.broadcast_to(lnb[None, :], (128, 128))),
        wup8=wup8, wdn8=wdn8,
        supt=np.ascontiguousarray(ups.T),
        sdnt=np.ascontiguousarray(dns.T),
        bupt=np.ascontiguousarray(bup.reshape(64, 128).T),
        bdnt=np.ascontiguousarray(bdn.reshape(16, 128).T),
        ula=ula_, ulb=ulb_, dla=dla_, dlb=dlb_,
        identb=np.eye(128, dtype=np.float32).astype(bf16),
        identf=np.eye(128, dtype=np.float32),
        tuones=np.ascontiguousarray(np.triu(np.ones((128, 128), np.float32))),
        masksu4=np.ascontiguousarray(
            np.tile(np.triu(np.ones((128, 128), np.float32), 1), (1, 4))).astype(bf16),
        maskiu4=np.ascontiguousarray(
            np.tile(np.triu(np.ones((128, 128), np.float32)), (1, 4))).astype(bf16),
        identb4=np.ascontiguousarray(
            np.tile(np.eye(128, dtype=np.float32), (1, 4))).astype(bf16),
        sel_up=_mk_sel_up(),
    )
    xfm = [np.ascontiguousarray(x[g].T) for g in range(2)]

    def per_core(core):
        g, j = core // 4, core % 4
        hsl = slice(512 * j, 512 * j + 512)
        d = dict(shared)
        d.update(
            xtm=x[g],
            xtmj=np.ascontiguousarray(x[g][TOK * j:TOK * j + TOK, :]),
            xfmj=np.ascontiguousarray(xfm[g][:, TOK * j:TOK * j + TOK]),
            wq=np.ascontiguousarray(Wq[:, hsl]),
            wk=np.ascontiguousarray(Wk[:, hsl]),
            wv=np.ascontiguousarray(Wv[:, hsl]),
            wg=np.ascontiguousarray(Wg[:, hsl]),
            wab=np.ascontiguousarray(np.concatenate(
                [Wa[:, 4 * j:4 * j + 4], Wb[:, 4 * j:4 * j + 4]], axis=1)),
            abb=np.ascontiguousarray(np.concatenate(
                [ba[4 * j:4 * j + 4], bb[4 * j:4 * j + 4]]).reshape(8, 1)),
            convw=np.ascontiguousarray(np.concatenate(
                [w[hsl].reshape(4, 128, KS).transpose(1, 0, 2).reshape(128, 4 * KS)
                 for w in (cqw, ckw, cvw)], axis=1)),
            convb=np.ascontiguousarray(np.concatenate(
                [b[hsl].reshape(4, 128).T for b in (cqb, ckb, cvb)], axis=1)),
            woj=np.ascontiguousarray(Wo[hsl, :]),
        )
        return d

    return per_core


def _get_nc():
    key = ("v1", tuple(n for n, _, _ in _TAPS))
    if key not in _NC_CACHE:
        _NC_CACHE[key] = build_nc(_TAPS)
    return _NC_CACHE[key]


def kernel(**inputs):
    per_core = _prep_core_inputs(inputs)
    nc = _get_nc()
    in_maps = [per_core(c) for c in range(8)]
    res = run_bass_kernel_spmd(nc, in_maps, core_ids=list(range(8)))
    out = np.empty((2, T, HID), dtype=np.float32)
    for core in range(8):
        g, j = core // 4, core % 4
        out[g, TOK * j:TOK * j + TOK, :] = np.asarray(res.results[core]["out"]).T
    if _TAPS:
        kernel._tap_results = [
            {n: np.asarray(res.results[c][n]) for n, _, _ in _TAPS} for c in range(8)]
    return out
